# version 14
# baseline (speedup 1.0000x reference)
import contextlib
import numpy as np
import concourse.bass as bass
import concourse.mybir as mybir
from concourse.bass_utils import run_bass_kernel_spmd

F32 = mybir.dt.float32
BF16 = mybir.dt.bfloat16
FP8 = mybir.dt.float8e4
AF = mybir.ActivationFunctionType
ALU = mybir.AluOpType
AX = mybir.AxisListType

P = 128
D = 2048
S = 2048
NT = D // P
NCORES = 8
E4M3_MAX = 448.0

ENGINES = ("sync", "gpsimd", "vector", "scalar", "tensor")
DMA_SEMS = (["rs0", "rs1", "rs2", "rs3", "rs4", "sv", "svb", "svc", "svd",
             "sw8r", "gc",
             "sq", "sq2", "sg2", "sg22", "so", "so2", "sxr", "sxr2",
             "gw", "gw2", "pv", "pv2", "pq", "pq2", "po", "po2",
             "pg", "pg2", "pout", "pout2", "px8", "px82", "pw8", "pw82"])
ORDER_SEMS = {"vector": "ov", "scalar": "os", "gpsimd": "og"}
SEMS = DMA_SEMS + ["tt", "tg", "ov", "os", "og"]

DEBUG = False


class Prog:

    def __init__(self):
        self.ops = {e: [] for e in ENGINES}
        self.counts = {s: 0 for s in SEMS}

    def op(self, eng, fn, waits=(), inc=None, n=1):
        waits = [w for w in waits if w is not None]
        osem = ORDER_SEMS.get(eng)
        if osem is not None:
            c = self.counts[osem]
            if c > 0:
                waits.append((osem, c))
            if inc is None or inc not in DMA_SEMS:
                self.counts[osem] = c + 1
                self.ops[eng].append((waits, fn, osem, 1))
                return (osem, c + 1)
        self.ops[eng].append((waits, fn, inc, n))
        if inc is not None:
            self.counts[inc] += n
            return (inc, self.counts[inc])
        return None

    def emit(self, eng_obj, eng_name, handles):
        seen = {}
        for waits, fn, inc, n in self.ops[eng_name]:
            for (s, c) in waits:
                if seen.get(s, 0) >= c:
                    continue
                eng_obj.wait_ge(handles[s], c)
                seen[s] = c
            inst = fn(eng_obj)
            if inc is not None:
                inst.then_inc(handles[inc], n)


def build_nc(debug=False):
    nc = bass.Bass("TRN2", target_bir_lowering=False)

    x_hbm = nc.dram_tensor("x", [S, D], F32, kind="ExternalInput").ap()
    wqp = nc.dram_tensor("wqp", [NT, P, D], F32, kind="ExternalInput").ap()
    wkp = nc.dram_tensor("wkp", [NT, P, D], F32, kind="ExternalInput").ap()
    wvp = nc.dram_tensor("wvp", [NT, P, D], F32, kind="ExternalInput").ap()
    wo_hbm = nc.dram_tensor("wo", [D, D], F32, kind="ExternalInput").ap()
    wft = nc.dram_tensor("wft", [D, D], BF16, kind="ExternalInput").ap()
    out_hbm = nc.dram_tensor("out", [S, D], F32, kind="ExternalOutput").ap()

    sk = "ExternalOutput" if debug else "Internal"
    v_sp = nc.dram_tensor("v_sp", [NT, P, S], BF16, kind=sk).ap()
    q_sp = nc.dram_tensor("q_sp", [NT, P, D], BF16, kind=sk).ap()
    o_sp = nc.dram_tensor("o_sp", [NT, P, S], BF16, kind=sk).ap()
    g_sp = nc.dram_tensor("g_sp", [NT, P, D], BF16, kind=sk).ap()
    x8_sp = nc.dram_tensor("x8_sp", [NT, P, D], FP8, kind=sk).ap()
    w8_sp = nc.dram_tensor("w8_sp", [NT, P, D], FP8, kind=sk).ap()
    red_sp = nc.dram_tensor("red_sp", [2, P, 1], F32).ap()
    cc_in = nc.dram_tensor("cc_in", [1, 1], F32).ap()
    cc_out = nc.dram_tensor("cc_out", [1, 1], F32).ap()
    bc_sp = nc.dram_tensor("bc_sp", [1, 4], F32).ap()
    if debug:
        dbg_xn = nc.dram_tensor("dbg_xn", [P, NT * S], BF16, kind="ExternalOutput").ap()
        dbg_st = nc.dram_tensor("dbg_st", [P, 192], F32, kind="ExternalOutput").ap()

    with contextlib.ExitStack() as stack:
        ec = stack.enter_context
        buf1 = ec(nc.sbuf_tensor("buf1", [P, NT * S], BF16))
        buf2 = ec(nc.sbuf_tensor("buf2", [P, NT * D], BF16))
        xtile0 = ec(nc.sbuf_tensor("xtile0", [P, D], F32))
        xtile1 = ec(nc.sbuf_tensor("xtile1", [P, D], F32))
        stage0 = ec(nc.sbuf_tensor("stage0", [P, D], BF16))
        stage1 = ec(nc.sbuf_tensor("stage1", [P, D], BF16))
        wpan0 = ec(nc.sbuf_tensor("wpan0", [P, D], BF16))
        wpan1 = ec(nc.sbuf_tensor("wpan1", [P, D], BF16))
        evac0 = ec(nc.sbuf_tensor("evac0", [P, D], BF16))
        evac1 = ec(nc.sbuf_tensor("evac1", [P, D], BF16))
        escr = ec(nc.sbuf_tensor("escr", [P, D], F32))
        ptm = ec(nc.sbuf_tensor("ptm", [P, D], BF16))
        pfm = ec(nc.sbuf_tensor("pfm", [P, D], BF16))
        outt0 = ec(nc.sbuf_tensor("outt0", [P, D], F32))
        outt1 = ec(nc.sbuf_tensor("outt1", [P, D], F32))
        stats = ec(nc.sbuf_tensor("stats", [P, 192], F32))
        ident = ec(nc.sbuf_tensor("ident", [P, P], BF16))
        redrow = ec(nc.sbuf_tensor("redrow", [1, 2 * P], F32))
        psA = ec(nc.psum_tensor("psA", [P, 2048], F32))
        psB = ec(nc.psum_tensor("psB", [P, 2048], F32))
        for _i in range(16):
            ec(nc.semaphore(f"pad{_i}"))
        handles = {s: ec(nc.semaphore(s)) for s in SEMS}
        nc.all_engine_barrier()
        lo = min(h.num for h in handles.values())
        nc.gpsimd.sem_clear(range(lo, bass.get_kernel_semaphore_range().stop))
        nc.all_engine_barrier()
        block = ec(nc.Block())
        pr = Prog()

        xt = [xtile0, xtile1]
        stg = [stage0, stage1]
        wpn = [wpan0, wpan1]
        evc = [evac0, evac1]
        ott = [outt0, outt1]
        pp = [ptm, pfm]
        ps = [psA, psB]
        psbit = [psA.ap().bitcast(BF16), psB.ap().bitcast(BF16)]
        evc8 = [evac0.ap().bitcast(FP8), evac1.ap().bitcast(FP8)]
        stg8 = [stage0.ap().bitcast(FP8), stage1.ap().bitcast(FP8)]
        escr_b = escr.ap().bitcast(BF16)
        buf1f8 = buf1.ap().bitcast(FP8)
        w8v = buf1f8[:, NT * S:2 * NT * S]
        ring = [xtile0, xtile1, outt0, outt1, escr]
        RD = len(ring)

        c_max = stats[:, 0:NT]
        c_min = stats[:, NT:2 * NT]
        c_sum = stats[:, 2 * NT:3 * NT]
        c_ssq = stats[:, 3 * NT:4 * NT]
        c_rstd = stats[:, 4 * NT:5 * NT]
        c_nmr = stats[:, 5 * NT:6 * NT]
        c_amax = stats[:, 6 * NT:7 * NT]
        c_qs = stats[:, 7 * NT:8 * NT]
        c_mean = stats[:, 8 * NT:9 * NT]
        c_qb = stats[:, 9 * NT:10 * NT]
        k0 = 10 * NT
        c_t0 = stats[:, k0:k0 + 1]
        c_recip = stats[:, k0 + 3:k0 + 4]
        c_bcx = stats[:, k0 + 4:k0 + 5]
        c_bcw = stats[:, k0 + 5:k0 + 6]
        c_bcg = stats[:, k0 + 6:k0 + 7]
        c_bc3 = stats[:, k0 + 4:k0 + 7]
        c_am1 = stats[:, k0 + 8:k0 + 9]
        c_wam1 = stats[:, k0 + 9:k0 + 10]
        c_wamp = stats[:, k0 + 16:k0 + 16 + NT]

        ev = {}

        def gsem(i):
            return "gw" if i % 2 == 0 else "gw2"

        def s2(base, i):
            return base if i % 2 == 0 else base + "2"

        pr.op("gpsimd", lambda g: g.memset(ident[:, :], 0.0))
        ev["ident"] = pr.op(
            "gpsimd",
            lambda g: g.affine_select(out=ident[:, :], in_=ident[:, :],
                                      compare_op=ALU.not_equal, fill=1.0, base=0,
                                      pattern=[[-1, P]], channel_multiplier=1),
            inc="tg")

        pan = {}
        for ft in range(2):
            pan[ft] = pr.op("gpsimd", (lambda ft: lambda e: e.dma_start(
                out=wpn[ft % 2][:, :], in_=wvp[ft]))(ft),
                waits=[], inc=gsem(ft), n=16)
        wfld = [None] * NT
        for c in range(NT):
            wfld[c] = pr.op("gpsimd", (lambda c: lambda e: e.dma_start(
                out=buf2[:, c * D:(c + 1) * D], in_=wft[c * P:(c + 1) * P, :]))(c),
                waits=[], inc=gsem(c), n=16)


        ring5 = [xtile0, xtile1, outt0, outt1, escr]
        vdone = [None] * NT
        sdone = [None] * NT
        apl = [None] * NT
        xn_tr = [None] * NT
        xnev = [None] * NT
        nmr_g = [None] * 4
        inv_d = float(np.float32(1.0 / D))
        used = {}

        def emit_c_group(ft, ns, extra):
            h = ft % 2
            mm = None
            for c in range(NT):
                w = list(extra) if c == 0 else []
                mm = pr.op("tensor", (lambda ft, ns, c, h: lambda e: e.matmul(
                    ps[h][:, ns * 512:(ns + 1) * 512],
                    wpn[ft % 2][:, c * P:(c + 1) * P],
                    buf1[:, c * S + ns * 512:c * S + (ns + 1) * 512],
                    start=(c == 0), stop=(c == NT - 1)))(ft, ns, c, h),
                    waits=w if c == 0 else (),
                    inc=("tt" if c == NT - 1 else None))
            return mm

        def emit_xnev(i):
            if i < 0 or xnev[i] is not None:
                return
            xnev[i] = pr.op("vector", (lambda i: lambda e: e.tensor_copy(
                buf1[:, :].rearrange("p (c s) -> p c s", c=NT)[:, :, i * P:(i + 1) * P],
                psbit[1][:, 0:D].rearrange("p (c t) -> p c t", c=NT)))(i),
                waits=[xn_tr[i]])

        def ln_chain(g):
            cs = slice(4 * g, 4 * g + 4)
            pr.op("vector", (lambda cs: lambda e: e.tensor_scalar_mul(
                c_mean[:, cs], c_sum[:, cs], inv_d))(cs),
                waits=[sdone[4 * g + 3]])
            pr.op("vector", (lambda cs: lambda e: e.tensor_mul(
                c_qb[:, cs], c_mean[:, cs], c_mean[:, cs]))(cs))
            pr.op("vector", (lambda cs: lambda e: e.tensor_scalar_mul(
                c_rstd[:, cs], c_ssq[:, cs], inv_d))(cs))
            pr.op("vector", (lambda cs: lambda e: e.tensor_sub(
                c_rstd[:, cs], c_rstd[:, cs], c_qb[:, cs]))(cs))
            var_ev = pr.op("vector", (lambda cs: lambda e: e.tensor_scalar_add(
                c_rstd[:, cs], c_rstd[:, cs], 1e-5))(cs))
            s_ev = pr.op("scalar", (lambda cs: lambda e: e.activation(
                c_qs[:, cs], c_rstd[:, cs], AF.Sqrt))(cs), waits=[var_ev])
            last = pr.op("vector", (lambda cs: lambda e: e.reciprocal(
                c_nmr[:, cs], c_qs[:, cs]))(cs), waits=[s_ev])
            for _ in range(2):
                pr.op("vector", (lambda cs: lambda e: e.tensor_mul(
                    c_qb[:, cs], c_nmr[:, cs], c_nmr[:, cs]))(cs), waits=[last])
                m2 = pr.op("vector", (lambda cs: lambda e: e.tensor_mul(
                    c_qb[:, cs], c_qb[:, cs], c_rstd[:, cs]))(cs))
                t_ev = pr.op("scalar", (lambda cs: lambda e: e.activation(
                    c_qb[:, cs], c_qb[:, cs], AF.Copy, scale=-0.5, bias=1.5))(cs),
                    waits=[m2])
                last = pr.op("vector", (lambda cs: lambda e: e.tensor_mul(
                    c_nmr[:, cs], c_nmr[:, cs], c_qb[:, cs]))(cs), waits=[t_ev])
            pr.op("vector", (lambda cs: lambda e: e.tensor_copy(
                c_rstd[:, cs], c_nmr[:, cs]))(cs), waits=[last])
            pr.op("vector", (lambda cs: lambda e: e.tensor_mul(
                c_nmr[:, cs], c_mean[:, cs], c_rstd[:, cs]))(cs))
            nmr_g[g] = pr.op("vector", (lambda cs: lambda e: e.tensor_scalar_mul(
                c_nmr[:, cs], c_nmr[:, cs], -1.0))(cs))
            pr.op("vector", (lambda cs: lambda e: e.tensor_sub(
                c_amax[:, cs], c_max[:, cs], c_mean[:, cs]))(cs))
            pr.op("vector", (lambda cs: lambda e: e.tensor_sub(
                c_qs[:, cs], c_mean[:, cs], c_min[:, cs]))(cs))
            pr.op("vector", (lambda cs: lambda e: e.tensor_max(
                c_amax[:, cs], c_amax[:, cs], c_qs[:, cs]))(cs))
            return pr.op("vector", (lambda cs: lambda e: e.tensor_mul(
                c_amax[:, cs], c_amax[:, cs], c_rstd[:, cs]))(cs))

        amx_ev = None
        for g in range(4):
            for i in range(4 * g, 4 * g + 4):
                b = i % 5
                w = [apl[i - 5]] if i >= 5 else []
                ld = pr.op("sync", (lambda i, b: lambda e: e.dma_start(
                    out=ring5[b][:, :], in_=x_hbm[i * P:(i + 1) * P, :]))(i, b),
                    waits=w, inc=f"rs{i % 2}", n=16)
                pr.op("vector", (lambda i, b: lambda e: e.reduce_max(
                    c_max[:, i:i + 1], ring5[b][:, :], axis=AX.X))(i, b),
                    waits=[ld])
                vdone[i] = pr.op("vector", (lambda i, b: lambda e: e.tensor_reduce(
                    c_min[:, i:i + 1], ring5[b][:, :], axis=AX.X, op=ALU.min))(i, b))
                pr.op("scalar", (lambda i, b: lambda e: e.activation(
                    ptm[:, :], ring5[b][:, :], AF.Copy,
                    accum_out=c_sum[:, i:i + 1]))(i, b), waits=[ld])
                sdone[i] = pr.op("scalar", (lambda i, b: lambda e: e.activation(
                    ptm[:, :], ring5[b][:, :], AF.Square,
                    accum_out=c_ssq[:, i:i + 1]))(i, b))
            amx_ev = ln_chain(g)
            for i in range(4 * g, 4 * g + 4):
                b = i % 5
                aw = [nmr_g[g]] + ([xn_tr[i - 2]] if i >= 2 else [])
                apl[i] = pr.op("scalar", (lambda i, b: lambda e: e.activation(
                    stg[i % 2][:, :], ring5[b][:, :], AF.Identity,
                    scale=c_rstd[:, i:i + 1], bias=c_nmr[:, i:i + 1]))(i, b),
                    waits=aw)
                emit_xnev(i - 1)
                tw = [apl[i], ev["ident"]] + ([xnev[i - 1]] if i >= 1 else [])
                tr = None
                for c in range(NT):
                    tr = pr.op("tensor", (lambda i, c: lambda e: e.transpose(
                        psbit[1][:, c * P:(c + 1) * P],
                        stg[i % 2][:, c * P:(c + 1) * P], ident[:, :]))(i, c),
                        waits=tw if c == 0 else (),
                        inc=("tt" if c == NT - 1 else None))
                xn_tr[i] = tr
            emit_xnev(4 * g + 3)
            used[0] = emit_c_group(0, g, [pan[0], xnev[4 * g + 3]])
        am_ev = pr.op("vector", lambda e: e.reduce_max(
            c_am1, c_amax, axis=AX.X), waits=[amx_ev])
        nmr_ev = nmr_g[3]
        if debug:
            pr.op("sync", lambda e: e.dma_start(out=dbg_st, in_=stats[:, :]),
                  waits=[am_ev, nmr_ev], inc="sw8r", n=16)

        d1 = pr.op("gpsimd", lambda e: e.dma_start(out=red_sp[0], in_=c_am1),
                   waits=[am_ev], inc="gc", n=16)
        d3a = pr.op("gpsimd", lambda e: e.dma_start(
            out=redrow[0:1, 0:P], in_=red_sp[0].rearrange("p o -> o p")),
            waits=[d1], inc="gc", n=16)
        xam2 = pr.op("vector", lambda e: e.reduce_max(
            stats[0:1, k0:k0 + 1], redrow[0:1, 0:P], axis=AX.X), waits=[d3a])
        wfred = [None] * NT
        for c in range(NT):
            wfred[c] = pr.op("vector", (lambda c: lambda e: e.reduce_max(
                c_wamp[:, c:c + 1], buf2[:, c * D:(c + 1) * D], axis=AX.X,
                apply_absolute_value=True))(c), waits=[wfld[c]])
        d4 = pr.op("gpsimd", lambda e: e.dma_start(
            out=cc_in, in_=stats[0:1, k0:k0 + 1]), waits=[xam2], inc="gc", n=16)
        cc_ev = pr.op("gpsimd", lambda e: e.collective_compute(
            "AllReduce", ALU.max, replica_groups=[list(range(NCORES))],
            ins=[cc_in], outs=[cc_out]), waits=[d4], inc="gc", n=1)
        d5 = pr.op("gpsimd", lambda e: e.dma_start(
            out=stats[0:1, k0:k0 + 1], in_=cc_out), waits=[cc_ev], inc="gc", n=16)
        inv_e4 = float(np.float32(1.0) / np.float32(E4M3_MAX))
        pr.op("vector", lambda e: e.tensor_scalar_mul(
            stats[0:1, k0:k0 + 1], stats[0:1, k0:k0 + 1], inv_e4), waits=[d5])
        pr.op("vector", lambda e: e.tensor_scalar_max(
            stats[0:1, k0:k0 + 1], stats[0:1, k0:k0 + 1], 1e-12))
        pr.op("vector", lambda e: e.tensor_scalar_mul(
            stats[0:1, k0 + 12:k0 + 13], stats[0:1, k0:k0 + 1], 2.0))
        sc_ev = pr.op("vector", lambda e: e.reciprocal(
            stats[0:1, k0 + 12:k0 + 13], stats[0:1, k0 + 12:k0 + 13]))
        d6 = pr.op("gpsimd", lambda e: e.dma_start(
            out=bc_sp[0:1, 0:1], in_=stats[0:1, k0 + 12:k0 + 13]),
            waits=[sc_ev], inc="gc", n=16)
        bc_ev = pr.op("gpsimd", lambda e: e.dma_start(
            out=c_bcx, in_=bc_sp[0:1, 0:1].broadcast_to([P, 1])),
            waits=[d6], inc="gc", n=16)
        pan[2] = pr.op("gpsimd", lambda e: e.dma_start(
            out=wpn[0][:, :], in_=wvp[2]), waits=[used[0]], inc=gsem(2), n=16)
        wamr = pr.op("vector", lambda e: e.reduce_max(
            c_wam1, c_wamp, axis=AX.X), waits=[wfred[NT - 1]])
        d2 = pr.op("gpsimd", lambda e: e.dma_start(out=red_sp[1], in_=c_wam1),
                   waits=[wamr], inc="gc", n=16)
        d3b = pr.op("gpsimd", lambda e: e.dma_start(
            out=redrow[0:1, P:2 * P], in_=red_sp[1].rearrange("p o -> o p")),
            waits=[d2], inc="gc", n=16)
        pr.op("vector", lambda e: e.reduce_max(
            stats[0:1, k0 + 1:k0 + 2], redrow[0:1, P:2 * P], axis=AX.X),
            waits=[d3b])
        pr.op("vector", lambda e: e.tensor_scalar_mul(
            stats[0:1, k0 + 1:k0 + 2], stats[0:1, k0 + 1:k0 + 2], inv_e4))
        pr.op("vector", lambda e: e.tensor_scalar_max(
            stats[0:1, k0 + 1:k0 + 2], stats[0:1, k0 + 1:k0 + 2], 1e-12))
        pr.op("vector", lambda e: e.tensor_mul(
            stats[0:1, k0 + 14:k0 + 15], stats[0:1, k0:k0 + 1],
            stats[0:1, k0 + 1:k0 + 2]))
        pr.op("vector", lambda e: e.tensor_scalar_mul(
            stats[0:1, k0 + 14:k0 + 15], stats[0:1, k0 + 14:k0 + 15], 4.0))
        pr.op("vector", lambda e: e.tensor_scalar_mul(
            stats[0:1, k0 + 13:k0 + 14], stats[0:1, k0 + 1:k0 + 2], 2.0))
        swc = pr.op("vector", lambda e: e.reciprocal(
            stats[0:1, k0 + 13:k0 + 14], stats[0:1, k0 + 13:k0 + 14]))
        d6w = pr.op("gpsimd", lambda e: e.dma_start(
            out=bc_sp[0:1, 1:3], in_=stats[0:1, k0 + 13:k0 + 15]),
            waits=[swc], inc="gc", n=16)
        ev["bcw"] = pr.op("gpsimd", lambda e: e.dma_start(
            out=stats[:, k0 + 5:k0 + 7],
            in_=bc_sp[0:1, 1:3].broadcast_to([P, 2])),
            waits=[d6w], inc="gc", n=16)

        ev["xn_done"] = xnev[NT - 1]
        if debug:
            pr.op("sync", lambda e: e.dma_start(out=dbg_xn, in_=buf1[:, :]),
                  waits=[ev["xn_done"]], inc="sw8r", n=16)

        freeH = {0: None, 1: None}
        vsp = [None] * NT
        eva0 = pr.op("scalar", lambda e: e.activation(
            evc[0][:, :], ps[0][:, :], AF.Copy), waits=[used[0]])
        freeH[0] = eva0
        vsp[0] = pr.op("scalar", lambda e: e.dma_start(
            out=v_sp[0], in_=evc[0][:, :]), waits=[eva0], inc="pv", n=16)

        wftre = [None] * NT
        w8q = [None] * NT
        w8spv = [None] * NT
        x8q = [None] * NT
        x8sp = [None] * NT
        ob8 = [outt0.ap().bitcast(FP8), outt1.ap().bitcast(FP8)]

        def emit_w8_band(c):
            b = c % 2
            w = [sdone[NT - 1]] if c < 2 else [w8q[c - 2]]
            wftre[c] = pr.op("gpsimd", (lambda c, b: lambda e: e.dma_start(
                out=pp[b][:, :], in_=wft[c * P:(c + 1) * P, :]))(c, b),
                waits=w, inc=gsem(c), n=16)
            qw = [wftre[c], ev["bcw"]]
            if c < 2:
                qw.append(xn_tr[NT - 2 + c])
            else:
                qw.append(w8spv[c - 2])
            w8q[c] = pr.op("vector", (lambda c, b: lambda e: e.tensor_scalar(
                stg8[b][:, 0:D], pp[b][:, :], c_bcw, None, op0=ALU.mult))(c, b),
                waits=qw)
            w8spv[c] = pr.op("scalar", (lambda c, b: lambda e: e.dma_start(
                out=w8_sp[c], in_=stg8[b][:, 0:D]))(c, b),
                waits=[w8q[c]], inc=s2("pw8", c), n=16)

        def emit_x8_tile(i):
            b = i % 2
            qw = [xnev[i], bc_ev]
            qw.append(apl[12 + b] if i < 2 else x8sp[i - 2])
            x8q[i] = pr.op("vector", (lambda i, b: lambda e: e.tensor_scalar(
                ob8[b][:, 0:D].rearrange("p (c t) -> p c t", c=NT),
                buf1[:, :].rearrange("p (c s) -> p c s", c=NT)[:, :, i * P:(i + 1) * P],
                c_bcx, None, op0=ALU.mult))(i, b), waits=qw)
            x8sp[i] = pr.op("scalar", (lambda i, b: lambda e: e.dma_start(
                out=x8_sp[i], in_=ob8[b][:, 0:D]))(i, b),
                waits=[x8q[i]], inc=s2("px8", i), n=16)

        for ft in range(1, NT):
            h = ft % 2
            extra = [pan[ft]]
            if ft == 1:
                extra += [xnev[NT - 1]]
            if freeH[h] is not None:
                extra.append(freeH[h])
            mm = None
            for ns in range(4):
                mm = emit_c_group(ft, ns, extra if ns == 0 else [])
            used[ft] = mm
            if ft + 2 < NT:
                pan[ft + 2] = pr.op("gpsimd", (lambda ft: lambda e: e.dma_start(
                    out=wpn[ft % 2][:, :], in_=wvp[ft + 2]))(ft),
                    waits=[mm], inc=gsem(ft), n=16)
            evw = [mm] + ([vsp[ft - 2]] if ft >= 2 else [])
            eva = pr.op("scalar", (lambda ft, h: lambda e: e.activation(
                evc[ft % 2][:, :], ps[h][:, :], AF.Copy))(ft, h), waits=evw)
            freeH[h] = eva
            vsp[ft] = pr.op("scalar", (lambda ft: lambda e: e.dma_start(
                out=v_sp[ft], in_=evc[ft % 2][:, :]))(ft), waits=[eva],
                inc=s2("pv", ft), n=16)
            emit_w8_band(ft - 1)
            emit_x8_tile(ft - 1)
        emit_w8_band(NT - 1)
        emit_x8_tile(NT - 1)

        def proj_band(ft, pan_ev, pan_buf, extra=(), mid=None):
            h = ft % 2
            mm = None
            for c in range(NT):
                w = [pan_ev, freeH[h]] + list(extra) if c == 0 else []
                for nsl in range(4):
                    mm = pr.op("tensor", (lambda c, nsl, h, pan_buf: lambda e: e.matmul(
                        ps[h][:, nsl * 512:(nsl + 1) * 512],
                        pan_buf[:, c * P:(c + 1) * P],
                        buf1[:, c * S + nsl * 512:c * S + (nsl + 1) * 512],
                        start=(c == 0), stop=(c == NT - 1)))(c, nsl, h, pan_buf),
                        waits=w if nsl == 0 else (),
                        inc=("tt" if (c == NT - 1 and nsl == 3) else None))
                if c == 2 and mid is not None:
                    mid()
            return mm

        qpan = {}
        qused = {}
        qsp = [None] * NT
        qtr = [None] * NT
        for ft in range(2):
            qpan[ft] = pr.op("gpsimd", (lambda ft: lambda e: e.dma_start(
                out=wpn[ft % 2][:, :], in_=wqp[ft]))(ft),
                waits=[used[NT - 2 + ft]], inc=gsem(ft), n=16)
        def d_bev(ft):
            h = ft % 2
            bw = [qused[ft]] + ([qtr[ft - 2]] if ft >= 2
                                else [w8spv[NT - 2 + ft]])
            return pr.op("scalar", (lambda ft, h: lambda e: e.activation(
                stg[ft % 2][:, :], ps[h][:, :], AF.Copy))(ft, h), waits=bw)

        def d_tail(ft, bev):
            h = ft % 2
            tr = None
            for c in range(NT):
                tr = pr.op("tensor", (lambda ft, c, h: lambda e: e.transpose(
                    psbit[h][:, c * P:(c + 1) * P],
                    stg[ft % 2][:, c * P:(c + 1) * P], ident[:, :]))(ft, c, h),
                    waits=[bev] if c == 0 else (),
                    inc=("tt" if c == NT - 1 else None))
            qtr[ft] = tr
            qe = pr.op("vector", (lambda ft, h: lambda e: e.tensor_copy(
                evc[ft % 2][:, :], psbit[h][:, 0:D]))(ft, h),
                waits=[tr] + ([qsp[ft - 2]] if ft >= 2 else [vsp[NT - 2 + ft]]))
            freeH[h] = qe
            qsp[ft] = pr.op("scalar", (lambda ft: lambda e: e.dma_start(
                out=q_sp[ft], in_=evc[ft % 2][:, :]))(ft), waits=[qe],
                inc=s2("pq", ft), n=16)

        dbev = {}
        for ft in range(NT):
            h = ft % 2
            mid = (lambda b=ft - 1: d_tail(b, dbev[b])) if ft >= 1 else None
            mm = proj_band(ft, qpan[ft], wpn[ft % 2], mid=mid)
            qused[ft] = mm
            dbev[ft] = d_bev(ft)
            if ft + 2 < NT:
                qpan[ft + 2] = pr.op("gpsimd", (lambda ft: lambda e: e.dma_start(
                    out=wpn[ft % 2][:, :], in_=wqp[ft + 2]))(ft),
                    waits=[mm], inc=gsem(ft), n=16)
        d_tail(NT - 1, dbev[NT - 1])

        kpan = {}
        kused = {}
        ktr = [None] * NT
        ktrv = [None] * NT
        for ft in range(2):
            kpan[ft] = pr.op("gpsimd", (lambda ft: lambda e: e.dma_start(
                out=wpn[ft % 2][:, :], in_=wkp[ft]))(ft),
                waits=[qused[NT - 2 + ft]], inc=gsem(ft), n=16)
        def e_bev(ft):
            h = ft % 2
            bw = [kused[ft]] + ([ktr[ft - 2]] if ft >= 2
                                else [qtr[NT - 2 + ft]])
            return pr.op("scalar", (lambda ft, h: lambda e: e.activation(
                stg[ft % 2][:, :], ps[h][:, :], AF.Copy))(ft, h), waits=bw)

        def e_tail(ft, bev):
            h = ft % 2
            tr = None
            for c in range(NT):
                tr = pr.op("tensor", (lambda ft, c, h: lambda e: e.transpose(
                    psbit[h][:, c * P:(c + 1) * P],
                    stg[ft % 2][:, c * P:(c + 1) * P], ident[:, :]))(ft, c, h),
                    waits=[bev] if c == 0 else (),
                    inc=("tt" if c == NT - 1 else None))
            ktr[ft] = tr
            ktrv[ft] = pr.op("vector", (lambda ft, h: lambda e: e.tensor_copy(
                buf2[:, :].rearrange("p (j f) -> p j f", j=NT)[:, :, ft * P:(ft + 1) * P],
                psbit[h][:, 0:D].rearrange("p (j t) -> p j t", j=NT)))(ft, h),
                waits=[tr] + ([qsp[NT - 2 + ft]] if ft < 2 else []))
            freeH[h] = ktrv[ft]

        ebev = {}
        for ft in range(NT):
            h = ft % 2
            mid = (lambda b=ft - 1: e_tail(b, ebev[b])) if ft >= 1 else None
            mm = proj_band(ft, kpan[ft], wpn[ft % 2], mid=mid)
            kused[ft] = mm
            ebev[ft] = e_bev(ft)
            if ft + 2 < NT:
                kpan[ft + 2] = pr.op("gpsimd", (lambda ft: lambda e: e.dma_start(
                    out=wpn[ft % 2][:, :], in_=wkp[ft + 2]))(ft),
                    waits=[mm], inc=gsem(ft), n=16)
        e_tail(NT - 1, ebev[NT - 1])
        ev["k_done"] = ktrv[NT - 1]

        st8 = [None] * 6
        for c in range(6):
            st8[c] = pr.op("gpsimd", (lambda c: lambda e: e.dma_start(
                out=ob8[c % 2][:, D + (c // 2) * D:D + (c // 2 + 1) * D],
                in_=w8_sp[c]))(c),
                waits=[w8spv[c], x8sp[NT - 2], x8sp[NT - 1]], inc="gc", n=16)

        qrl = [None] * NT
        qrl[0] = pr.op("sync", lambda e: e.dma_start(
            out=ptm[:, :], in_=q_sp[0]),
            waits=[qsp[0], w8q[NT - 2]], inc="sq", n=16)
        qrl[1] = pr.op("sync", lambda e: e.dma_start(
            out=wpn[1][:, :], in_=q_sp[1]),
            waits=[qsp[1], kused[NT - 1]], inc="sq2", n=16)

        vgrp = [None] * 4
        vg_sems = ["sv", "svb", "svc", "svd"]
        vg_eng = ["sync", "scalar", "gpsimd", "sync"]
        for k in range(4):
            vgrp[k] = pr.op(vg_eng[k], (lambda k: lambda e: e.dma_start(
                out=buf1[:, 4 * k * S:(4 * k + 4) * S].rearrange(
                    "p (c s) -> p c s", c=4),
                in_=v_sp[4 * k:4 * k + 4].rearrange("c p s -> p c s")))(k),
                waits=[vsp[4 * k + 3], vsp[4 * k + 2], kused[NT - 1],
                       x8q[NT - 1]],
                inc=vg_sems[k], n=16)
        ev["v_loaded"] = vgrp[3]

        def q_buf(et):
            return ptm if et == 0 else wpn[et % 2]

        def emit_scores(et, extra):
            h = et % 2
            mm = None
            for j in range(NT):
                w = [qrl[et], ev["k_done"]] + list(extra) if j == 0 else []
                for nsl in range(4):
                    mm = pr.op("tensor", (lambda et, j, nsl, h: lambda e: e.matmul(
                        ps[h][:, nsl * 512:(nsl + 1) * 512],
                        q_buf(et)[:, j * P:(j + 1) * P],
                        buf2[:, j * D + nsl * 512:j * D + (nsl + 1) * 512],
                        start=(j == 0), stop=(j == NT - 1)))(et, j, nsl, h),
                        waits=w if nsl == 0 else (),
                        inc=("tt" if (j == NT - 1 and nsl == 3) else None))
            return mm

        sc_done = [None] * NT
        exp_done = [None] * NT
        osp = [None] * NT
        oev = [None] * NT
        o_done = [None] * NT

        sc_done[0] = emit_scores(0, [freeH[0]])
        sc_done[1] = emit_scores(1, [freeH[1]])

        for et in range(NT):
            h = et % 2
            b = et % 2
            vm = pr.op("vector", (lambda h: lambda e: e.reduce_max(
                c_t0, ps[h][:, :], axis=AX.X, negate=True))(h),
                waits=[sc_done[et]])
            vb = pr.op("vector", lambda e: e.tensor_scalar_mul(c_t0, c_t0, 0.125))
            exp_done[et] = pr.op("scalar", (lambda h: lambda e: e.activation(
                escr[:, :], ps[h][:, :], AF.Exp, bias=c_t0, scale=0.125,
                accum_out=c_recip))(h), waits=[vb])
            rc = pr.op("vector", lambda e: e.reciprocal(c_recip, c_recip),
                       waits=[exp_done[et]])
            ptm_ev = pr.op("vector", lambda e: e.tensor_copy(ptm[:, :], escr[:, :]),
                           waits=[exp_done[et]])
            tr = None
            for c in range(NT):
                tr = pr.op("tensor", (lambda c, h: lambda e: e.transpose(
                    psbit[h][:, c * P:(c + 1) * P],
                    ptm[:, c * P:(c + 1) * P], ident[:, :]))(c, h),
                    waits=[ptm_ev] if c == 0 else (),
                    inc=("tt" if c == NT - 1 else None))
            pfm_ev = pr.op("vector", (lambda h: lambda e: e.tensor_copy(
                pfm[:, :], psbit[h][:, 0:D]))(h), waits=[tr])
            mm2 = None
            for c in range(NT):
                w = []
                if c == 0:
                    w = [pfm_ev, vgrp[0]]
                elif c % 4 == 0:
                    w = [vgrp[c // 4]]
                for nsl in range(4):
                    mm2 = pr.op("tensor", (lambda c, nsl, h: lambda e: e.matmul(
                        ps[h][:, nsl * 512:(nsl + 1) * 512],
                        pfm[:, c * P:(c + 1) * P],
                        buf1[:, c * S + nsl * 512:c * S + (nsl + 1) * 512],
                        start=(c == 0), stop=(c == NT - 1)))(c, nsl, h),
                        waits=w if nsl == 0 else (),
                        inc=("tt" if (c == NT - 1 and nsl == 3) else None))
            o_done[et] = mm2
            oev[et] = pr.op("scalar", (lambda b, h: lambda e: e.activation(
                evc[b][:, :], ps[h][:, :], AF.Copy, scale=c_recip))(b, h),
                waits=[rc, mm2] + ([osp[et - 2]] if et >= 2 else [qsp[NT - 2 + et]]))
            osp[et] = pr.op("scalar", (lambda et, b: lambda e: e.dma_start(
                out=o_sp[:, :, et * P:(et + 1) * P].rearrange("j p s -> p j s"),
                in_=evc[b][:, :].rearrange("p (j s) -> p j s", j=NT)))(et, b),
                inc=s2("po", et), n=16)
            if et + 2 < NT:
                qrl[et + 2] = pr.op("sync", (lambda et: lambda e: e.dma_start(
                    out=wpn[et % 2][:, :], in_=q_sp[et + 2]))(et),
                    waits=[qsp[et + 2], sc_done[et]], inc=s2("sq", et), n=16)
                freeH[et % 2] = oev[et]
                sc_done[et + 2] = emit_scores(et + 2, [freeH[et % 2]])

        w8rA = pr.op("sync", lambda e: e.dma_start(
            out=w8v[:, 6 * D:11 * D].rearrange("p (c j) -> p c j", c=5),
            in_=w8_sp[6:11].rearrange("c p j -> p c j")),
            waits=[o_done[NT - 1], w8spv[NT - 2], w8spv[NT - 1]], inc="sw8r", n=16)
        w8rB = pr.op("gpsimd", lambda e: e.dma_start(
            out=w8v[:, 11 * D:16 * D].rearrange("p (c j) -> p c j", c=5),
            in_=w8_sp[11:16].rearrange("c p j -> p c j")),
            waits=[o_done[NT - 1], w8spv[NT - 2], w8spv[NT - 1]], inc="gc", n=16)
        x8rld = [None] * NT
        ffn_mm = [None] * NT
        ffn_free = [None] * NT
        gsp = [None] * NT
        for i in range(2):
            x8rld[i] = pr.op("sync", (lambda i: lambda e: e.dma_start(
                out=evc8[i % 2][:, 0:D], in_=x8_sp[i]))(i),
                waits=[osp[NT - 2 + i], x8sp[i]],
                inc=("sxr" if i % 2 == 0 else "sxr2"), n=16)
        wo_ev = [None] * NT
        for c in range(NT):
            w = [sc_done[NT - 1]] if c < 2 else [wo_ev[c - 2]]
            wo_ev[c] = pr.op("gpsimd", (lambda c: lambda e: e.dma_start(
                out=buf2[:, c * D:(c + 1) * D], in_=wo_hbm[c * P:(c + 1) * P, :]))(c),
                waits=w, inc=gsem(c), n=16)
        ev["wo_done"] = wo_ev[NT - 1]
        ev["wo_done2"] = wo_ev[NT - 2]

        def w8rhs(c, nsl):
            if c < 6:
                return ob8[c % 2][:, D + (c // 2) * D + nsl * 512:
                                  D + (c // 2) * D + (nsl + 1) * 512]
            return w8v[:, c * D + nsl * 512:c * D + (nsl + 1) * 512]

        for i in range(NT):
            h = i % 2
            mm = None
            for c in range(NT):
                w = []
                if c == 0:
                    w = [x8rld[i]]
                    w.append(oev[NT - 2 + i] if i < 2 else ffn_free[i - 2])
                if i == 0:
                    if c == 0:
                        w += [st8[5]]
                    elif c == 6:
                        w += [w8rA]
                    elif c == 11:
                        w += [w8rB]
                for nsl in range(4):
                    mm = pr.op("tensor", (lambda i, c, nsl, h: lambda e: e.matmul(
                        ps[h][:, nsl * 512:(nsl + 1) * 512],
                        evc8[i % 2][:, c * P:(c + 1) * P],
                        w8rhs(c, nsl),
                        start=(c == 0), stop=(c == NT - 1)))(i, c, nsl, h),
                        waits=w if nsl == 0 else (),
                        inc=("tt" if (c == NT - 1 and nsl == 3) else None))
            ffn_mm[i] = mm
            if i + 2 < NT:
                x8rld[i + 2] = pr.op("sync", (lambda i: lambda e: e.dma_start(
                    out=evc8[i % 2][:, 0:D], in_=x8_sp[i + 2]))(i),
                    waits=[mm, x8sp[i + 2]],
                    inc=("sxr" if i % 2 == 0 else "sxr2"), n=16)
            o1 = pr.op("scalar", (lambda h: lambda e: e.activation(
                escr_b[:, 0:D], ps[h][:, :], AF.Copy, scale=c_bcg))(h),
                waits=[mm, ev["bcw"]])
            ffn_free[i] = o1
            sg = pr.op("scalar", lambda e: e.activation(
                escr_b[:, D:2 * D], escr_b[:, 0:D], AF.Sigmoid))
            gm = pr.op("vector", (lambda i: lambda e: e.tensor_mul(
                stg[i % 2][:, :], escr_b[:, 0:D], escr_b[:, D:2 * D]))(i),
                waits=[sg] + ([gsp[i - 2]] if i >= 2 else []))
            gsp[i] = pr.op("scalar", (lambda i: lambda e: e.dma_start(
                out=g_sp[i], in_=stg[i % 2][:, :]))(i), waits=[gm],
                inc=s2("pg", i), n=16)

        orl = [None] * NT
        grl = [None] * NT
        xrl = [None] * NT
        for st in range(2):
            orl[st] = pr.op("sync", (lambda st: lambda e: e.dma_start(
                out=wpn[st % 2][:, :], in_=o_sp[st]))(st),
                waits=[osp[NT - 1], sc_done[NT - 1]], inc=s2("so", st), n=16)
            grl[st] = pr.op("sync", (lambda st: lambda e: e.dma_start(
                out=evc[st % 2][:, :], in_=g_sp[st]))(st),
                waits=[gsp[st], ffn_mm[NT - 2 + st]], inc=s2("sg2", st), n=16)
            xg = 15 if st == 0 else 11
            xrl[st] = pr.op("sync", (lambda st: lambda e: e.dma_start(
                out=xt[st % 2][:, :], in_=x_hbm[st * P:(st + 1) * P, :]))(st),
                waits=[apl[xg]], inc=f"rs{st % 2}", n=16)
        out_ev = [None] * NT
        vadd = [None] * NT
        wo_free = [None] * NT
        for st in range(NT):
            h = st % 2
            b = st % 2
            mm = None
            for c in range(NT):
                w = []
                if c == 0:
                    w = [orl[st], ev["wo_done"], ev["wo_done2"]]
                    w.append(ffn_free[NT - 2 + st] if st < 2 else wo_free[st - 2])
                for nsl in range(4):
                    mm = pr.op("tensor", (lambda st, c, nsl, h, b: lambda e: e.matmul(
                        ps[h][:, nsl * 512:(nsl + 1) * 512],
                        wpn[b][:, c * P:(c + 1) * P],
                        buf2[:, c * D + nsl * 512:c * D + (nsl + 1) * 512],
                        start=(c == 0), stop=(c == NT - 1)))(st, c, nsl, h, b),
                        waits=w if nsl == 0 else (),
                        inc=("tt" if (c == NT - 1 and nsl == 3) else None))
            sgm = pr.op("vector", (lambda b, h: lambda e: e.tensor_mul(
                escr[:, :], ps[h][:, :], evc[b][:, :]))(b, h),
                waits=[mm, grl[st]] + ([ffn_mm[NT - 1]] if st == 0 else []))
            wo_free[st] = sgm
            vadd[st] = pr.op("vector", (lambda st, b: lambda e: e.tensor_add(
                ott[b][:, :], escr[:, :], xt[b][:, :]))(st, b),
                waits=[xrl[st]] + ([out_ev[st - 2]] if st >= 2
                                   else [apl[12 + st], x8sp[14 + st],
                                         ffn_mm[NT - 1]]))
            out_ev[st] = pr.op("scalar", (lambda st, b: lambda e: e.dma_start(
                out=out_hbm[st * P:(st + 1) * P, :], in_=ott[b][:, :]))(st, b),
                waits=[vadd[st]], inc=s2("pout", st), n=16)
            if st + 2 < NT:
                orl[st + 2] = pr.op("sync", (lambda st, b: lambda e: e.dma_start(
                    out=wpn[b][:, :], in_=o_sp[st + 2]))(st, b),
                    waits=[mm], inc=s2("so", st), n=16)
                grl[st + 2] = pr.op("sync", (lambda st, b: lambda e: e.dma_start(
                    out=evc[b][:, :], in_=g_sp[st + 2]))(st, b),
                    waits=[sgm], inc=s2("sg2", st), n=16)
                xrl[st + 2] = pr.op("sync", (lambda st, b: lambda e: e.dma_start(
                    out=xt[b][:, :], in_=x_hbm[(st + 2) * P:(st + 3) * P, :]))(st, b),
                    waits=[vadd[st]], inc=f"rs{st % 2}", n=16)

        final_waits = [("pout", pr.counts["pout"]), ("pout2", pr.counts["pout2"])]
        if debug:
            for s in ("pv", "pv2", "pq", "pq2", "po", "po2", "pg", "pg2",
                      "px8", "px82", "pw8", "pw82"):
                final_waits.append((s, pr.counts[s]))
        pr.op("sync", lambda e: e.nop(), waits=final_waits)

        @block.sync
        def _(e):
            pr.emit(e, "sync", handles)

        @block.gpsimd
        def _(e):
            pr.emit(e, "gpsimd", handles)

        @block.vector
        def _(e):
            pr.emit(e, "vector", handles)

        @block.scalar
        def _(e):
            pr.emit(e, "scalar", handles)

        @block.tensor
        def _(e):
            pr.emit(e, "tensor", handles)

    return nc


def _layout_inputs(L0_attn_x, L0_attn_Wq, L0_attn_Wk, L0_attn_Wv, L0_attn_Wo,
                   L0_ffn_weight):
    def panelize(w):
        return np.ascontiguousarray(
            np.asarray(w).reshape(NT, P, NT, P).transpose(2, 1, 0, 3).reshape(NT, P, D))

    wqp = panelize(L0_attn_Wq)
    wkp = panelize(L0_attn_Wk)
    wvp = panelize(L0_attn_Wv)
    wo = np.ascontiguousarray(np.asarray(L0_attn_Wo))
    wft = np.ascontiguousarray(np.asarray(L0_ffn_weight).T)
    xs = np.asarray(L0_attn_x)
    return [{"x": np.ascontiguousarray(xs[c]), "wqp": wqp, "wkp": wkp,
             "wvp": wvp, "wo": wo, "wft": wft} for c in range(NCORES)]


def kernel(L0_attn_x, L0_attn_Wq, L0_attn_Wk, L0_attn_Wv, L0_attn_Wo,
           L0_ffn_weight):
    in_maps = _layout_inputs(L0_attn_x, L0_attn_Wq, L0_attn_Wk, L0_attn_Wv,
                             L0_attn_Wo, L0_ffn_weight)
    nc = build_nc(debug=DEBUG)
    res = run_bass_kernel_spmd(nc, in_maps, core_ids=list(range(NCORES)))
    return np.stack([res.results[c]["out"] for c in range(NCORES)], axis=0)


# version 15
# speedup vs baseline: 1.0043x; 1.0043x over previous
import contextlib
import numpy as np
import concourse.bass as bass
import concourse.mybir as mybir
from concourse.bass_utils import run_bass_kernel_spmd

F32 = mybir.dt.float32
BF16 = mybir.dt.bfloat16
FP8 = mybir.dt.float8e4
AF = mybir.ActivationFunctionType
ALU = mybir.AluOpType
AX = mybir.AxisListType

P = 128
D = 2048
S = 2048
NT = D // P
NCORES = 8
E4M3_MAX = 448.0

ENGINES = ("sync", "gpsimd", "vector", "scalar", "tensor")
DMA_SEMS = (["rs0", "rs1", "rs2", "rs3", "rs4", "sv", "svb", "svc", "svd",
             "sw8r", "gc",
             "sq", "sq2", "sg2", "sg22", "so", "so2", "sxr", "sxr2",
             "gw", "gw2", "pv", "pv2", "pq", "pq2", "po", "po2",
             "pg", "pg2", "pout", "pout2", "px8", "px82", "pw8", "pw82"])
ORDER_SEMS = {"vector": "ov", "scalar": "os", "gpsimd": "og"}
SEMS = DMA_SEMS + ["tt", "tg", "ov", "os", "og"]

DEBUG = False


class Prog:

    def __init__(self):
        self.ops = {e: [] for e in ENGINES}
        self.counts = {s: 0 for s in SEMS}

    def op(self, eng, fn, waits=(), inc=None, n=1):
        waits = [w for w in waits if w is not None]
        osem = ORDER_SEMS.get(eng)
        if osem is not None:
            c = self.counts[osem]
            if c > 0:
                waits.append((osem, c))
            if inc is None or inc not in DMA_SEMS:
                self.counts[osem] = c + 1
                self.ops[eng].append((waits, fn, osem, 1))
                return (osem, c + 1)
        self.ops[eng].append((waits, fn, inc, n))
        if inc is not None:
            self.counts[inc] += n
            return (inc, self.counts[inc])
        return None

    def emit(self, eng_obj, eng_name, handles):
        seen = {}
        for waits, fn, inc, n in self.ops[eng_name]:
            for (s, c) in waits:
                if seen.get(s, 0) >= c:
                    continue
                eng_obj.wait_ge(handles[s], c)
                seen[s] = c
            inst = fn(eng_obj)
            if inc is not None:
                inst.then_inc(handles[inc], n)


def build_nc(debug=False):
    nc = bass.Bass("TRN2", target_bir_lowering=False)

    x_hbm = nc.dram_tensor("x", [S, D], F32, kind="ExternalInput").ap()
    wqp = nc.dram_tensor("wqp", [NT, P, D], F32, kind="ExternalInput").ap()
    wkp = nc.dram_tensor("wkp", [NT, P, D], F32, kind="ExternalInput").ap()
    wvp = nc.dram_tensor("wvp", [NT, P, D], F32, kind="ExternalInput").ap()
    wo_hbm = nc.dram_tensor("wo", [D, D], F32, kind="ExternalInput").ap()
    wft = nc.dram_tensor("wft", [D, D], BF16, kind="ExternalInput").ap()
    out_hbm = nc.dram_tensor("out", [S, D], F32, kind="ExternalOutput").ap()

    sk = "ExternalOutput" if debug else "Internal"
    v_sp = nc.dram_tensor("v_sp", [NT, P, S], BF16, kind=sk).ap()
    q_sp = nc.dram_tensor("q_sp", [NT, P, D], BF16, kind=sk).ap()
    o_sp = nc.dram_tensor("o_sp", [NT, P, S], BF16, kind=sk).ap()
    g_sp = nc.dram_tensor("g_sp", [NT, P, D], BF16, kind=sk).ap()
    x8_sp = nc.dram_tensor("x8_sp", [NT, P, D], FP8, kind=sk).ap()
    w8_sp = nc.dram_tensor("w8_sp", [NT, P, D], FP8, kind=sk).ap()
    red_sp = nc.dram_tensor("red_sp", [2, P, 1], F32).ap()
    cc_in = nc.dram_tensor("cc_in", [1, 1], F32).ap()
    cc_out = nc.dram_tensor("cc_out", [1, 1], F32).ap()
    bc_sp = nc.dram_tensor("bc_sp", [1, 4], F32).ap()
    if debug:
        dbg_xn = nc.dram_tensor("dbg_xn", [P, NT * S], BF16, kind="ExternalOutput").ap()
        dbg_st = nc.dram_tensor("dbg_st", [P, 192], F32, kind="ExternalOutput").ap()

    with contextlib.ExitStack() as stack:
        ec = stack.enter_context
        buf1 = ec(nc.sbuf_tensor("buf1", [P, NT * S], BF16))
        buf2 = ec(nc.sbuf_tensor("buf2", [P, NT * D], BF16))
        xtile0 = ec(nc.sbuf_tensor("xtile0", [P, D], F32))
        xtile1 = ec(nc.sbuf_tensor("xtile1", [P, D], F32))
        stage0 = ec(nc.sbuf_tensor("stage0", [P, D], BF16))
        stage1 = ec(nc.sbuf_tensor("stage1", [P, D], BF16))
        wpan0 = ec(nc.sbuf_tensor("wpan0", [P, D], BF16))
        wpan1 = ec(nc.sbuf_tensor("wpan1", [P, D], BF16))
        evac0 = ec(nc.sbuf_tensor("evac0", [P, D], BF16))
        evac1 = ec(nc.sbuf_tensor("evac1", [P, D], BF16))
        escr = ec(nc.sbuf_tensor("escr", [P, D], F32))
        ptm = ec(nc.sbuf_tensor("ptm", [P, D], BF16))
        pfm = ec(nc.sbuf_tensor("pfm", [P, D], BF16))
        outt0 = ec(nc.sbuf_tensor("outt0", [P, D], F32))
        outt1 = ec(nc.sbuf_tensor("outt1", [P, D], F32))
        stats = ec(nc.sbuf_tensor("stats", [P, 192], F32))
        ident = ec(nc.sbuf_tensor("ident", [P, P], BF16))
        redrow = ec(nc.sbuf_tensor("redrow", [1, 2 * P], F32))
        psA = ec(nc.psum_tensor("psA", [P, 2048], F32))
        psB = ec(nc.psum_tensor("psB", [P, 2048], F32))
        handles = {s: ec(nc.semaphore(s)) for s in SEMS}
        nc.all_engine_barrier()
        lo = min(h.num for h in handles.values())
        nc.gpsimd.sem_clear(range(lo, bass.get_kernel_semaphore_range().stop))
        nc.all_engine_barrier()
        block = ec(nc.Block())
        pr = Prog()

        xt = [xtile0, xtile1]
        stg = [stage0, stage1]
        wpn = [wpan0, wpan1]
        evc = [evac0, evac1]
        ott = [outt0, outt1]
        pp = [ptm, pfm]
        ps = [psA, psB]
        psbit = [psA.ap().bitcast(BF16), psB.ap().bitcast(BF16)]
        evc8 = [evac0.ap().bitcast(FP8), evac1.ap().bitcast(FP8)]
        stg8 = [stage0.ap().bitcast(FP8), stage1.ap().bitcast(FP8)]
        escr_b = escr.ap().bitcast(BF16)
        buf1f8 = buf1.ap().bitcast(FP8)
        w8v = buf1f8[:, NT * S:2 * NT * S]
        ring = [xtile0, xtile1, outt0, outt1, escr]
        RD = len(ring)

        c_max = stats[:, 0:NT]
        c_min = stats[:, NT:2 * NT]
        c_sum = stats[:, 2 * NT:3 * NT]
        c_ssq = stats[:, 3 * NT:4 * NT]
        c_rstd = stats[:, 4 * NT:5 * NT]
        c_nmr = stats[:, 5 * NT:6 * NT]
        c_amax = stats[:, 6 * NT:7 * NT]
        c_qs = stats[:, 7 * NT:8 * NT]
        c_mean = stats[:, 8 * NT:9 * NT]
        c_qb = stats[:, 9 * NT:10 * NT]
        k0 = 10 * NT
        c_t0 = stats[:, k0:k0 + 1]
        c_recip = stats[:, k0 + 3:k0 + 4]
        c_bcx = stats[:, k0 + 4:k0 + 5]
        c_bcw = stats[:, k0 + 5:k0 + 6]
        c_bcg = stats[:, k0 + 6:k0 + 7]
        c_bc3 = stats[:, k0 + 4:k0 + 7]
        c_am1 = stats[:, k0 + 8:k0 + 9]
        c_wam1 = stats[:, k0 + 9:k0 + 10]
        c_wamp = stats[:, k0 + 16:k0 + 16 + NT]

        ev = {}

        def gsem(i):
            return "gw" if i % 2 == 0 else "gw2"

        def s2(base, i):
            return base if i % 2 == 0 else base + "2"

        pr.op("gpsimd", lambda g: g.memset(ident[:, :], 0.0))
        ev["ident"] = pr.op(
            "gpsimd",
            lambda g: g.affine_select(out=ident[:, :], in_=ident[:, :],
                                      compare_op=ALU.not_equal, fill=1.0, base=0,
                                      pattern=[[-1, P]], channel_multiplier=1),
            inc="tg")

        pan = {}
        for ft in range(2):
            pan[ft] = pr.op("gpsimd", (lambda ft: lambda e: e.dma_start(
                out=wpn[ft % 2][:, :], in_=wvp[ft]))(ft),
                waits=[], inc=gsem(ft), n=16)
        wfld = [None] * NT
        for c in range(NT):
            wfld[c] = pr.op("gpsimd", (lambda c: lambda e: e.dma_start(
                out=buf2[:, c * D:(c + 1) * D], in_=wft[c * P:(c + 1) * P, :]))(c),
                waits=[], inc=gsem(c), n=16)


        ring5 = [xtile0, xtile1, outt0, outt1, escr]
        vdone = [None] * NT
        sdone = [None] * NT
        apl = [None] * NT
        xn_tr = [None] * NT
        xnev = [None] * NT
        nmr_g = [None] * 4
        inv_d = float(np.float32(1.0 / D))
        used = {}

        def emit_c_group(ft, ns, extra):
            h = ft % 2
            mm = None
            for c in range(NT):
                w = list(extra) if c == 0 else []
                mm = pr.op("tensor", (lambda ft, ns, c, h: lambda e: e.matmul(
                    ps[h][:, ns * 512:(ns + 1) * 512],
                    wpn[ft % 2][:, c * P:(c + 1) * P],
                    buf1[:, c * S + ns * 512:c * S + (ns + 1) * 512],
                    start=(c == 0), stop=(c == NT - 1)))(ft, ns, c, h),
                    waits=w if c == 0 else (),
                    inc=("tt" if c == NT - 1 else None))
            return mm

        def emit_xnev(i):
            if i < 0 or xnev[i] is not None:
                return
            xnev[i] = pr.op("vector", (lambda i: lambda e: e.tensor_copy(
                buf1[:, :].rearrange("p (c s) -> p c s", c=NT)[:, :, i * P:(i + 1) * P],
                psbit[1][:, 0:D].rearrange("p (c t) -> p c t", c=NT)))(i),
                waits=[xn_tr[i]])

        def ln_chain(g):
            cs = slice(4 * g, 4 * g + 4)
            pr.op("vector", (lambda cs: lambda e: e.tensor_scalar_mul(
                c_mean[:, cs], c_sum[:, cs], inv_d))(cs),
                waits=[sdone[4 * g + 3]])
            pr.op("vector", (lambda cs: lambda e: e.tensor_mul(
                c_qb[:, cs], c_mean[:, cs], c_mean[:, cs]))(cs))
            pr.op("vector", (lambda cs: lambda e: e.tensor_scalar_mul(
                c_rstd[:, cs], c_ssq[:, cs], inv_d))(cs))
            pr.op("vector", (lambda cs: lambda e: e.tensor_sub(
                c_rstd[:, cs], c_rstd[:, cs], c_qb[:, cs]))(cs))
            var_ev = pr.op("vector", (lambda cs: lambda e: e.tensor_scalar_add(
                c_rstd[:, cs], c_rstd[:, cs], 1e-5))(cs))
            s_ev = pr.op("scalar", (lambda cs: lambda e: e.activation(
                c_qs[:, cs], c_rstd[:, cs], AF.Sqrt))(cs), waits=[var_ev])
            last = pr.op("vector", (lambda cs: lambda e: e.reciprocal(
                c_nmr[:, cs], c_qs[:, cs]))(cs), waits=[s_ev])
            for _ in range(2):
                pr.op("vector", (lambda cs: lambda e: e.tensor_mul(
                    c_qb[:, cs], c_nmr[:, cs], c_nmr[:, cs]))(cs), waits=[last])
                m2 = pr.op("vector", (lambda cs: lambda e: e.tensor_mul(
                    c_qb[:, cs], c_qb[:, cs], c_rstd[:, cs]))(cs))
                t_ev = pr.op("scalar", (lambda cs: lambda e: e.activation(
                    c_qb[:, cs], c_qb[:, cs], AF.Copy, scale=-0.5, bias=1.5))(cs),
                    waits=[m2])
                last = pr.op("vector", (lambda cs: lambda e: e.tensor_mul(
                    c_nmr[:, cs], c_nmr[:, cs], c_qb[:, cs]))(cs), waits=[t_ev])
            pr.op("vector", (lambda cs: lambda e: e.tensor_copy(
                c_rstd[:, cs], c_nmr[:, cs]))(cs), waits=[last])
            pr.op("vector", (lambda cs: lambda e: e.tensor_mul(
                c_nmr[:, cs], c_mean[:, cs], c_rstd[:, cs]))(cs))
            nmr_g[g] = pr.op("vector", (lambda cs: lambda e: e.tensor_scalar_mul(
                c_nmr[:, cs], c_nmr[:, cs], -1.0))(cs))
            pr.op("vector", (lambda cs: lambda e: e.tensor_sub(
                c_amax[:, cs], c_max[:, cs], c_mean[:, cs]))(cs))
            pr.op("vector", (lambda cs: lambda e: e.tensor_sub(
                c_qs[:, cs], c_mean[:, cs], c_min[:, cs]))(cs))
            pr.op("vector", (lambda cs: lambda e: e.tensor_max(
                c_amax[:, cs], c_amax[:, cs], c_qs[:, cs]))(cs))
            return pr.op("vector", (lambda cs: lambda e: e.tensor_mul(
                c_amax[:, cs], c_amax[:, cs], c_rstd[:, cs]))(cs))

        pan_pxy = pr.op("vector", lambda e: e.tensor_copy(
            redrow[0:1, 0:1], redrow[0:1, 1:2]), waits=[pan[0], pan[1]])
        amx_ev = None
        for g in range(4):
            for i in range(4 * g, 4 * g + 4):
                b = i % 5
                w = [apl[i - 5]] if i >= 5 else []
                ld = pr.op("sync", (lambda i, b: lambda e: e.dma_start(
                    out=ring5[b][:, :], in_=x_hbm[i * P:(i + 1) * P, :]))(i, b),
                    waits=w, inc=f"rs{i % 2}", n=16)
                pr.op("vector", (lambda i, b: lambda e: e.reduce_max(
                    c_max[:, i:i + 1], ring5[b][:, :], axis=AX.X))(i, b),
                    waits=[ld])
                vdone[i] = pr.op("vector", (lambda i, b: lambda e: e.tensor_reduce(
                    c_min[:, i:i + 1], ring5[b][:, :], axis=AX.X, op=ALU.min))(i, b))
                pr.op("scalar", (lambda i, b: lambda e: e.activation(
                    ptm[:, :], ring5[b][:, :], AF.Copy,
                    accum_out=c_sum[:, i:i + 1]))(i, b), waits=[ld])
                sdone[i] = pr.op("scalar", (lambda i, b: lambda e: e.activation(
                    ptm[:, :], ring5[b][:, :], AF.Square,
                    accum_out=c_ssq[:, i:i + 1]))(i, b))
            amx_ev = ln_chain(g)
            for i in range(4 * g, 4 * g + 4):
                b = i % 5
                aw = [nmr_g[g]] + ([xn_tr[i - 2]] if i >= 2 else [])
                apl[i] = pr.op("scalar", (lambda i, b: lambda e: e.activation(
                    stg[i % 2][:, :], ring5[b][:, :], AF.Identity,
                    scale=c_rstd[:, i:i + 1], bias=c_nmr[:, i:i + 1]))(i, b),
                    waits=aw)
                emit_xnev(i - 1)
                tw = [apl[i], ev["ident"]] + ([xnev[i - 1]] if i >= 1 else [])
                tr = None
                for c in range(NT):
                    tr = pr.op("tensor", (lambda i, c: lambda e: e.transpose(
                        psbit[1][:, c * P:(c + 1) * P],
                        stg[i % 2][:, c * P:(c + 1) * P], ident[:, :]))(i, c),
                        waits=tw if c == 0 else (),
                        inc=("tt" if c == NT - 1 else None))
                xn_tr[i] = tr
            emit_xnev(4 * g + 3)
            used[0] = emit_c_group(0, g, [pan[0], xnev[4 * g + 3]])
        am_ev = pr.op("vector", lambda e: e.reduce_max(
            c_am1, c_amax, axis=AX.X), waits=[amx_ev])
        nmr_ev = nmr_g[3]
        if debug:
            pr.op("sync", lambda e: e.dma_start(out=dbg_st, in_=stats[:, :]),
                  waits=[am_ev, nmr_ev], inc="sw8r", n=16)

        d1 = pr.op("gpsimd", lambda e: e.dma_start(out=red_sp[0], in_=c_am1),
                   waits=[am_ev], inc="gc", n=16)
        d3a = pr.op("gpsimd", lambda e: e.dma_start(
            out=redrow[0:1, 0:P], in_=red_sp[0].rearrange("p o -> o p")),
            waits=[d1], inc="gc", n=16)
        xam2 = pr.op("vector", lambda e: e.reduce_max(
            stats[0:1, k0:k0 + 1], redrow[0:1, 0:P], axis=AX.X), waits=[d3a])
        wfred = [None] * NT
        for c in range(NT):
            wfred[c] = pr.op("vector", (lambda c: lambda e: e.reduce_max(
                c_wamp[:, c:c + 1], buf2[:, c * D:(c + 1) * D], axis=AX.X,
                apply_absolute_value=True))(c), waits=[wfld[c]])
        d4 = pr.op("gpsimd", lambda e: e.dma_start(
            out=cc_in, in_=stats[0:1, k0:k0 + 1]), waits=[xam2], inc="gc", n=16)
        cc_ev = pr.op("gpsimd", lambda e: e.collective_compute(
            "AllReduce", ALU.max, replica_groups=[list(range(NCORES))],
            ins=[cc_in], outs=[cc_out]), waits=[d4], inc="gc", n=1)
        d5 = pr.op("gpsimd", lambda e: e.dma_start(
            out=stats[0:1, k0:k0 + 1], in_=cc_out), waits=[cc_ev], inc="gc", n=16)
        inv_e4 = float(np.float32(1.0) / np.float32(E4M3_MAX))
        pr.op("vector", lambda e: e.tensor_scalar_mul(
            stats[0:1, k0:k0 + 1], stats[0:1, k0:k0 + 1], inv_e4), waits=[d5])
        pr.op("vector", lambda e: e.tensor_scalar_max(
            stats[0:1, k0:k0 + 1], stats[0:1, k0:k0 + 1], 1e-12))
        pr.op("vector", lambda e: e.tensor_scalar_mul(
            stats[0:1, k0 + 12:k0 + 13], stats[0:1, k0:k0 + 1], 2.0))
        sc_ev = pr.op("vector", lambda e: e.reciprocal(
            stats[0:1, k0 + 12:k0 + 13], stats[0:1, k0 + 12:k0 + 13]))
        d6 = pr.op("gpsimd", lambda e: e.dma_start(
            out=bc_sp[0:1, 0:1], in_=stats[0:1, k0 + 12:k0 + 13]),
            waits=[sc_ev], inc="gc", n=16)
        bc_ev = pr.op("gpsimd", lambda e: e.dma_start(
            out=c_bcx, in_=bc_sp[0:1, 0:1].broadcast_to([P, 1])),
            waits=[d6], inc="gc", n=16)
        pan[2] = pr.op("gpsimd", lambda e: e.dma_start(
            out=wpn[0][:, :], in_=wvp[2]), waits=[used[0]], inc=gsem(2), n=16)
        wamr = pr.op("vector", lambda e: e.reduce_max(
            c_wam1, c_wamp, axis=AX.X), waits=[wfred[NT - 1]])
        d2 = pr.op("gpsimd", lambda e: e.dma_start(out=red_sp[1], in_=c_wam1),
                   waits=[wamr], inc="gc", n=16)
        d3b = pr.op("gpsimd", lambda e: e.dma_start(
            out=redrow[0:1, P:2 * P], in_=red_sp[1].rearrange("p o -> o p")),
            waits=[d2], inc="gc", n=16)
        pr.op("vector", lambda e: e.reduce_max(
            stats[0:1, k0 + 1:k0 + 2], redrow[0:1, P:2 * P], axis=AX.X),
            waits=[d3b])
        pr.op("vector", lambda e: e.tensor_scalar_mul(
            stats[0:1, k0 + 1:k0 + 2], stats[0:1, k0 + 1:k0 + 2], inv_e4))
        pr.op("vector", lambda e: e.tensor_scalar_max(
            stats[0:1, k0 + 1:k0 + 2], stats[0:1, k0 + 1:k0 + 2], 1e-12))
        pr.op("vector", lambda e: e.tensor_mul(
            stats[0:1, k0 + 14:k0 + 15], stats[0:1, k0:k0 + 1],
            stats[0:1, k0 + 1:k0 + 2]))
        pr.op("vector", lambda e: e.tensor_scalar_mul(
            stats[0:1, k0 + 14:k0 + 15], stats[0:1, k0 + 14:k0 + 15], 4.0))
        pr.op("vector", lambda e: e.tensor_scalar_mul(
            stats[0:1, k0 + 13:k0 + 14], stats[0:1, k0 + 1:k0 + 2], 2.0))
        swc = pr.op("vector", lambda e: e.reciprocal(
            stats[0:1, k0 + 13:k0 + 14], stats[0:1, k0 + 13:k0 + 14]))
        d6w = pr.op("gpsimd", lambda e: e.dma_start(
            out=bc_sp[0:1, 1:3], in_=stats[0:1, k0 + 13:k0 + 15]),
            waits=[swc], inc="gc", n=16)
        ev["bcw"] = pr.op("gpsimd", lambda e: e.dma_start(
            out=stats[:, k0 + 5:k0 + 7],
            in_=bc_sp[0:1, 1:3].broadcast_to([P, 2])),
            waits=[d6w], inc="gc", n=16)

        ev["xn_done"] = xnev[NT - 1]
        if debug:
            pr.op("sync", lambda e: e.dma_start(out=dbg_xn, in_=buf1[:, :]),
                  waits=[ev["xn_done"]], inc="sw8r", n=16)

        freeH = {0: None, 1: None}
        vsp = [None] * NT
        eva0 = pr.op("scalar", lambda e: e.activation(
            evc[0][:, :], ps[0][:, :], AF.Copy), waits=[used[0]])
        freeH[0] = eva0
        vsp[0] = pr.op("scalar", lambda e: e.dma_start(
            out=v_sp[0], in_=evc[0][:, :]), waits=[eva0], inc="pv", n=16)

        wftre = [None] * NT
        w8q = [None] * NT
        w8spv = [None] * NT
        x8q = [None] * NT
        x8sp = [None] * NT
        ob8 = [outt0.ap().bitcast(FP8), outt1.ap().bitcast(FP8)]

        def emit_w8_band(c):
            b = c % 2
            w = [sdone[NT - 1]] if c < 2 else [w8q[c - 2]]
            wftre[c] = pr.op("gpsimd", (lambda c, b: lambda e: e.dma_start(
                out=pp[b][:, :], in_=wft[c * P:(c + 1) * P, :]))(c, b),
                waits=w, inc=gsem(c), n=16)
            qw = [wftre[c], ev["bcw"]]
            if c < 2:
                qw.append(xn_tr[NT - 2 + c])
            else:
                qw.append(w8spv[c - 2])
            w8q[c] = pr.op("vector", (lambda c, b: lambda e: e.tensor_scalar(
                stg8[b][:, 0:D], pp[b][:, :], c_bcw, None, op0=ALU.mult))(c, b),
                waits=qw)
            w8spv[c] = pr.op("scalar", (lambda c, b: lambda e: e.dma_start(
                out=w8_sp[c], in_=stg8[b][:, 0:D]))(c, b),
                waits=[w8q[c]], inc=s2("pw8", c), n=16)

        def emit_x8_tile(i):
            b = i % 2
            qw = [xnev[i], bc_ev]
            qw.append(apl[12 + b] if i < 2 else x8sp[i - 2])
            x8q[i] = pr.op("vector", (lambda i, b: lambda e: e.tensor_scalar(
                ob8[b][:, 0:D].rearrange("p (c t) -> p c t", c=NT),
                buf1[:, :].rearrange("p (c s) -> p c s", c=NT)[:, :, i * P:(i + 1) * P],
                c_bcx, None, op0=ALU.mult))(i, b), waits=qw)
            x8sp[i] = pr.op("scalar", (lambda i, b: lambda e: e.dma_start(
                out=x8_sp[i], in_=ob8[b][:, 0:D]))(i, b),
                waits=[x8q[i]], inc=s2("px8", i), n=16)

        for ft in range(1, NT):
            h = ft % 2
            extra = [pan_pxy if ft == 1 else pan[ft]]
            if ft == 1:
                extra += [xnev[NT - 1]]
            if freeH[h] is not None:
                extra.append(freeH[h])
            mm = None
            for ns in range(4):
                mm = emit_c_group(ft, ns, extra if ns == 0 else [])
            used[ft] = mm
            if ft + 2 < NT:
                pan[ft + 2] = pr.op("gpsimd", (lambda ft: lambda e: e.dma_start(
                    out=wpn[ft % 2][:, :], in_=wvp[ft + 2]))(ft),
                    waits=[mm], inc=gsem(ft), n=16)
            evw = [mm] + ([vsp[ft - 2]] if ft >= 2 else [])
            eva = pr.op("scalar", (lambda ft, h: lambda e: e.activation(
                evc[ft % 2][:, :], ps[h][:, :], AF.Copy))(ft, h), waits=evw)
            freeH[h] = eva
            vsp[ft] = pr.op("scalar", (lambda ft: lambda e: e.dma_start(
                out=v_sp[ft], in_=evc[ft % 2][:, :]))(ft), waits=[eva],
                inc=s2("pv", ft), n=16)
            emit_w8_band(ft - 1)
            emit_x8_tile(ft - 1)
        emit_w8_band(NT - 1)
        emit_x8_tile(NT - 1)

        def proj_band(ft, pan_ev, pan_buf, extra=(), mid=None):
            h = ft % 2
            mm = None
            for c in range(NT):
                w = [pan_ev, freeH[h]] + list(extra) if c == 0 else []
                for nsl in range(4):
                    mm = pr.op("tensor", (lambda c, nsl, h, pan_buf: lambda e: e.matmul(
                        ps[h][:, nsl * 512:(nsl + 1) * 512],
                        pan_buf[:, c * P:(c + 1) * P],
                        buf1[:, c * S + nsl * 512:c * S + (nsl + 1) * 512],
                        start=(c == 0), stop=(c == NT - 1)))(c, nsl, h, pan_buf),
                        waits=w if nsl == 0 else (),
                        inc=("tt" if (c == NT - 1 and nsl == 3) else None))
                if c == 2 and mid is not None:
                    mid()
            return mm

        qpan = {}
        qused = {}
        qsp = [None] * NT
        qtr = [None] * NT
        for ft in range(2):
            qpan[ft] = pr.op("gpsimd", (lambda ft: lambda e: e.dma_start(
                out=wpn[ft % 2][:, :], in_=wqp[ft]))(ft),
                waits=[used[NT - 2 + ft]], inc=gsem(ft), n=16)
        def d_bev(ft):
            h = ft % 2
            bw = [qused[ft]] + ([qtr[ft - 2]] if ft >= 2
                                else [w8spv[NT - 2 + ft]])
            return pr.op("scalar", (lambda ft, h: lambda e: e.activation(
                stg[ft % 2][:, :], ps[h][:, :], AF.Copy))(ft, h), waits=bw)

        def d_tail(ft, bev):
            h = ft % 2
            tr = None
            for c in range(NT):
                tr = pr.op("tensor", (lambda ft, c, h: lambda e: e.transpose(
                    psbit[h][:, c * P:(c + 1) * P],
                    stg[ft % 2][:, c * P:(c + 1) * P], ident[:, :]))(ft, c, h),
                    waits=[bev] if c == 0 else (),
                    inc=("tt" if c == NT - 1 else None))
            qtr[ft] = tr
            qe = pr.op("vector", (lambda ft, h: lambda e: e.tensor_copy(
                evc[ft % 2][:, :], psbit[h][:, 0:D]))(ft, h),
                waits=[tr] + ([qsp[ft - 2]] if ft >= 2 else [vsp[NT - 2 + ft]]))
            freeH[h] = qe
            qsp[ft] = pr.op("scalar", (lambda ft: lambda e: e.dma_start(
                out=q_sp[ft], in_=evc[ft % 2][:, :]))(ft), waits=[qe],
                inc=s2("pq", ft), n=16)

        dbev = {}
        for ft in range(NT):
            h = ft % 2
            mid = (lambda b=ft - 1: d_tail(b, dbev[b])) if ft >= 1 else None
            mm = proj_band(ft, qpan[ft], wpn[ft % 2], mid=mid)
            qused[ft] = mm
            dbev[ft] = d_bev(ft)
            if ft + 2 < NT:
                qpan[ft + 2] = pr.op("gpsimd", (lambda ft: lambda e: e.dma_start(
                    out=wpn[ft % 2][:, :], in_=wqp[ft + 2]))(ft),
                    waits=[mm], inc=gsem(ft), n=16)
        d_tail(NT - 1, dbev[NT - 1])

        kpan = {}
        kused = {}
        ktr = [None] * NT
        ktrv = [None] * NT
        for ft in range(2):
            kpan[ft] = pr.op("gpsimd", (lambda ft: lambda e: e.dma_start(
                out=wpn[ft % 2][:, :], in_=wkp[ft]))(ft),
                waits=[qused[NT - 2 + ft]], inc=gsem(ft), n=16)
        def e_bev(ft):
            h = ft % 2
            bw = [kused[ft]] + ([ktr[ft - 2]] if ft >= 2
                                else [qtr[NT - 2 + ft]])
            return pr.op("scalar", (lambda ft, h: lambda e: e.activation(
                stg[ft % 2][:, :], ps[h][:, :], AF.Copy))(ft, h), waits=bw)

        def e_tail(ft, bev):
            h = ft % 2
            tr = None
            for c in range(NT):
                tr = pr.op("tensor", (lambda ft, c, h: lambda e: e.transpose(
                    psbit[h][:, c * P:(c + 1) * P],
                    stg[ft % 2][:, c * P:(c + 1) * P], ident[:, :]))(ft, c, h),
                    waits=[bev] if c == 0 else (),
                    inc=("tt" if c == NT - 1 else None))
            ktr[ft] = tr
            ktrv[ft] = pr.op("vector", (lambda ft, h: lambda e: e.tensor_copy(
                buf2[:, :].rearrange("p (j f) -> p j f", j=NT)[:, :, ft * P:(ft + 1) * P],
                psbit[h][:, 0:D].rearrange("p (j t) -> p j t", j=NT)))(ft, h),
                waits=[tr] + ([qsp[NT - 2 + ft]] if ft < 2 else []))
            freeH[h] = ktrv[ft]

        ebev = {}
        for ft in range(NT):
            h = ft % 2
            mid = (lambda b=ft - 1: e_tail(b, ebev[b])) if ft >= 1 else None
            mm = proj_band(ft, kpan[ft], wpn[ft % 2], mid=mid)
            kused[ft] = mm
            ebev[ft] = e_bev(ft)
            if ft + 2 < NT:
                kpan[ft + 2] = pr.op("gpsimd", (lambda ft: lambda e: e.dma_start(
                    out=wpn[ft % 2][:, :], in_=wkp[ft + 2]))(ft),
                    waits=[mm], inc=gsem(ft), n=16)
        e_tail(NT - 1, ebev[NT - 1])
        ev["k_done"] = ktrv[NT - 1]

        st8 = [None] * 6
        for c in range(6):
            st8[c] = pr.op("gpsimd", (lambda c: lambda e: e.dma_start(
                out=ob8[c % 2][:, D + (c // 2) * D:D + (c // 2 + 1) * D],
                in_=w8_sp[c]))(c),
                waits=[w8spv[c], x8sp[NT - 2], x8sp[NT - 1]], inc="gc", n=16)

        qrl = [None] * NT
        qrl[0] = pr.op("sync", lambda e: e.dma_start(
            out=ptm[:, :], in_=q_sp[0]),
            waits=[qsp[0], w8q[NT - 2]], inc="sq", n=16)
        qrl[1] = pr.op("sync", lambda e: e.dma_start(
            out=wpn[1][:, :], in_=q_sp[1]),
            waits=[qsp[1], kused[NT - 1]], inc="sq2", n=16)

        vgrp = [None] * 4
        vg_sems = ["sv", "svb", "svc", "svd"]
        vg_eng = ["sync", "scalar", "gpsimd", "sync"]
        for k in range(4):
            vgrp[k] = pr.op(vg_eng[k], (lambda k: lambda e: e.dma_start(
                out=buf1[:, 4 * k * S:(4 * k + 4) * S].rearrange(
                    "p (c s) -> p c s", c=4),
                in_=v_sp[4 * k:4 * k + 4].rearrange("c p s -> p c s")))(k),
                waits=[vsp[4 * k + 3], vsp[4 * k + 2], kused[NT - 1],
                       x8q[NT - 1]],
                inc=vg_sems[k], n=16)
        ev["v_loaded"] = vgrp[3]

        def q_buf(et):
            return ptm if et == 0 else wpn[et % 2]

        def emit_scores(et, extra):
            h = et % 2
            mm = None
            for j in range(NT):
                w = [qrl[et], ev["k_done"]] + list(extra) if j == 0 else []
                for nsl in range(4):
                    mm = pr.op("tensor", (lambda et, j, nsl, h: lambda e: e.matmul(
                        ps[h][:, nsl * 512:(nsl + 1) * 512],
                        q_buf(et)[:, j * P:(j + 1) * P],
                        buf2[:, j * D + nsl * 512:j * D + (nsl + 1) * 512],
                        start=(j == 0), stop=(j == NT - 1)))(et, j, nsl, h),
                        waits=w if nsl == 0 else (),
                        inc=("tt" if (j == NT - 1 and nsl == 3) else None))
            return mm

        sc_done = [None] * NT
        exp_done = [None] * NT
        osp = [None] * NT
        oev = [None] * NT
        o_done = [None] * NT

        sc_done[0] = emit_scores(0, [freeH[0]])
        sc_done[1] = emit_scores(1, [freeH[1]])

        for et in range(NT):
            h = et % 2
            b = et % 2
            vm = pr.op("vector", (lambda h: lambda e: e.reduce_max(
                c_t0, ps[h][:, :], axis=AX.X, negate=True))(h),
                waits=[sc_done[et]])
            vb = pr.op("vector", lambda e: e.tensor_scalar_mul(c_t0, c_t0, 0.125))
            exp_done[et] = pr.op("scalar", (lambda h: lambda e: e.activation(
                escr[:, :], ps[h][:, :], AF.Exp, bias=c_t0, scale=0.125,
                accum_out=c_recip))(h), waits=[vb])
            rc = pr.op("vector", lambda e: e.reciprocal(c_recip, c_recip),
                       waits=[exp_done[et]])
            ptm_ev = pr.op("vector", lambda e: e.tensor_copy(ptm[:, :], escr[:, :]),
                           waits=[exp_done[et]])
            tr = None
            for c in range(NT):
                tr = pr.op("tensor", (lambda c, h: lambda e: e.transpose(
                    psbit[h][:, c * P:(c + 1) * P],
                    ptm[:, c * P:(c + 1) * P], ident[:, :]))(c, h),
                    waits=[ptm_ev] if c == 0 else (),
                    inc=("tt" if c == NT - 1 else None))
            pfm_ev = pr.op("vector", (lambda h: lambda e: e.tensor_copy(
                pfm[:, :], psbit[h][:, 0:D]))(h), waits=[tr])
            mm2 = None
            for c in range(NT):
                w = []
                if c == 0:
                    w = [pfm_ev, vgrp[0]]
                elif c % 4 == 0:
                    w = [vgrp[c // 4]]
                for nsl in range(4):
                    mm2 = pr.op("tensor", (lambda c, nsl, h: lambda e: e.matmul(
                        ps[h][:, nsl * 512:(nsl + 1) * 512],
                        pfm[:, c * P:(c + 1) * P],
                        buf1[:, c * S + nsl * 512:c * S + (nsl + 1) * 512],
                        start=(c == 0), stop=(c == NT - 1)))(c, nsl, h),
                        waits=w if nsl == 0 else (),
                        inc=("tt" if (c == NT - 1 and nsl == 3) else None))
            o_done[et] = mm2
            oev[et] = pr.op("scalar", (lambda b, h: lambda e: e.activation(
                evc[b][:, :], ps[h][:, :], AF.Copy, scale=c_recip))(b, h),
                waits=[rc, mm2] + ([osp[et - 2]] if et >= 2 else [qsp[NT - 2 + et]]))
            osp[et] = pr.op("scalar", (lambda et, b: lambda e: e.dma_start(
                out=o_sp[:, :, et * P:(et + 1) * P].rearrange("j p s -> p j s"),
                in_=evc[b][:, :].rearrange("p (j s) -> p j s", j=NT)))(et, b),
                inc=s2("po", et), n=16)
            if et + 2 < NT:
                qrl[et + 2] = pr.op("sync", (lambda et: lambda e: e.dma_start(
                    out=wpn[et % 2][:, :], in_=q_sp[et + 2]))(et),
                    waits=[qsp[et + 2], sc_done[et]], inc=s2("sq", et), n=16)
                freeH[et % 2] = oev[et]
                sc_done[et + 2] = emit_scores(et + 2, [freeH[et % 2]])

        w8rA = pr.op("sync", lambda e: e.dma_start(
            out=w8v[:, 6 * D:11 * D].rearrange("p (c j) -> p c j", c=5),
            in_=w8_sp[6:11].rearrange("c p j -> p c j")),
            waits=[o_done[NT - 1], w8spv[NT - 2], w8spv[NT - 1]], inc="sw8r", n=16)
        w8rB = pr.op("gpsimd", lambda e: e.dma_start(
            out=w8v[:, 11 * D:16 * D].rearrange("p (c j) -> p c j", c=5),
            in_=w8_sp[11:16].rearrange("c p j -> p c j")),
            waits=[o_done[NT - 1], w8spv[NT - 2], w8spv[NT - 1]], inc="gc", n=16)
        x8rld = [None] * NT
        ffn_mm = [None] * NT
        ffn_free = [None] * NT
        gsp = [None] * NT
        for i in range(2):
            x8rld[i] = pr.op("sync", (lambda i: lambda e: e.dma_start(
                out=evc8[i % 2][:, 0:D], in_=x8_sp[i]))(i),
                waits=[osp[NT - 2 + i], x8sp[i]],
                inc=("sxr" if i % 2 == 0 else "sxr2"), n=16)
        wo_ev = [None] * NT
        for c in range(NT):
            w = [sc_done[NT - 1]] if c < 2 else [wo_ev[c - 2]]
            wo_ev[c] = pr.op("gpsimd", (lambda c: lambda e: e.dma_start(
                out=buf2[:, c * D:(c + 1) * D], in_=wo_hbm[c * P:(c + 1) * P, :]))(c),
                waits=w, inc=gsem(c), n=16)
        ev["wo_done"] = wo_ev[NT - 1]
        ev["wo_done2"] = wo_ev[NT - 2]

        def w8rhs(c, nsl):
            if c < 6:
                return ob8[c % 2][:, D + (c // 2) * D + nsl * 512:
                                  D + (c // 2) * D + (nsl + 1) * 512]
            return w8v[:, c * D + nsl * 512:c * D + (nsl + 1) * 512]

        for i in range(NT):
            h = i % 2
            mm = None
            for c in range(NT):
                w = []
                if c == 0:
                    w = [x8rld[i]]
                    w.append(oev[NT - 2 + i] if i < 2 else ffn_free[i - 2])
                if i == 0:
                    if c == 0:
                        w += [st8[5]]
                    elif c == 6:
                        w += [w8rA]
                    elif c == 11:
                        w += [w8rB]
                for nsl in range(4):
                    mm = pr.op("tensor", (lambda i, c, nsl, h: lambda e: e.matmul(
                        ps[h][:, nsl * 512:(nsl + 1) * 512],
                        evc8[i % 2][:, c * P:(c + 1) * P],
                        w8rhs(c, nsl),
                        start=(c == 0), stop=(c == NT - 1)))(i, c, nsl, h),
                        waits=w if nsl == 0 else (),
                        inc=("tt" if (c == NT - 1 and nsl == 3) else None))
            ffn_mm[i] = mm
            if i + 2 < NT:
                x8rld[i + 2] = pr.op("sync", (lambda i: lambda e: e.dma_start(
                    out=evc8[i % 2][:, 0:D], in_=x8_sp[i + 2]))(i),
                    waits=[mm, x8sp[i + 2]],
                    inc=("sxr" if i % 2 == 0 else "sxr2"), n=16)
            o1 = pr.op("scalar", (lambda h: lambda e: e.activation(
                escr_b[:, 0:D], ps[h][:, :], AF.Copy, scale=c_bcg))(h),
                waits=[mm, ev["bcw"]])
            ffn_free[i] = o1
            sg = pr.op("scalar", lambda e: e.activation(
                escr_b[:, D:2 * D], escr_b[:, 0:D], AF.Sigmoid))
            gm = pr.op("vector", (lambda i: lambda e: e.tensor_mul(
                stg[i % 2][:, :], escr_b[:, 0:D], escr_b[:, D:2 * D]))(i),
                waits=[sg] + ([gsp[i - 2]] if i >= 2 else []))
            gsp[i] = pr.op("scalar", (lambda i: lambda e: e.dma_start(
                out=g_sp[i], in_=stg[i % 2][:, :]))(i), waits=[gm],
                inc=s2("pg", i), n=16)

        orl = [None] * NT
        grl = [None] * NT
        xrl = [None] * NT
        for st in range(2):
            orl[st] = pr.op("sync", (lambda st: lambda e: e.dma_start(
                out=wpn[st % 2][:, :], in_=o_sp[st]))(st),
                waits=[osp[NT - 1], sc_done[NT - 1]], inc=s2("so", st), n=16)
            grl[st] = pr.op("sync", (lambda st: lambda e: e.dma_start(
                out=evc[st % 2][:, :], in_=g_sp[st]))(st),
                waits=[gsp[st], ffn_mm[NT - 2 + st]], inc=s2("sg2", st), n=16)
            xg = 15 if st == 0 else 11
            xrl[st] = pr.op("sync", (lambda st: lambda e: e.dma_start(
                out=xt[st % 2][:, :], in_=x_hbm[st * P:(st + 1) * P, :]))(st),
                waits=[apl[xg]], inc=f"rs{st % 2}", n=16)
        out_ev = [None] * NT
        vadd = [None] * NT
        wo_free = [None] * NT
        for st in range(NT):
            h = st % 2
            b = st % 2
            mm = None
            for c in range(NT):
                w = []
                if c == 0:
                    w = [orl[st], ev["wo_done"], ev["wo_done2"]]
                    w.append(ffn_free[NT - 2 + st] if st < 2 else wo_free[st - 2])
                for nsl in range(4):
                    mm = pr.op("tensor", (lambda st, c, nsl, h, b: lambda e: e.matmul(
                        ps[h][:, nsl * 512:(nsl + 1) * 512],
                        wpn[b][:, c * P:(c + 1) * P],
                        buf2[:, c * D + nsl * 512:c * D + (nsl + 1) * 512],
                        start=(c == 0), stop=(c == NT - 1)))(st, c, nsl, h, b),
                        waits=w if nsl == 0 else (),
                        inc=("tt" if (c == NT - 1 and nsl == 3) else None))
            sgm = pr.op("vector", (lambda b, h: lambda e: e.tensor_mul(
                escr[:, :], ps[h][:, :], evc[b][:, :]))(b, h),
                waits=[mm, grl[st]] + ([ffn_mm[NT - 1]] if st == 0 else []))
            wo_free[st] = sgm
            vadd[st] = pr.op("vector", (lambda st, b: lambda e: e.tensor_add(
                ott[b][:, :], escr[:, :], xt[b][:, :]))(st, b),
                waits=[xrl[st]] + ([out_ev[st - 2]] if st >= 2
                                   else [apl[12 + st], x8sp[14 + st],
                                         ffn_mm[NT - 1]]))
            out_ev[st] = pr.op("scalar", (lambda st, b: lambda e: e.dma_start(
                out=out_hbm[st * P:(st + 1) * P, :], in_=ott[b][:, :]))(st, b),
                waits=[vadd[st]], inc=s2("pout", st), n=16)
            if st + 2 < NT:
                orl[st + 2] = pr.op("sync", (lambda st, b: lambda e: e.dma_start(
                    out=wpn[b][:, :], in_=o_sp[st + 2]))(st, b),
                    waits=[mm], inc=s2("so", st), n=16)
                grl[st + 2] = pr.op("sync", (lambda st, b: lambda e: e.dma_start(
                    out=evc[b][:, :], in_=g_sp[st + 2]))(st, b),
                    waits=[sgm], inc=s2("sg2", st), n=16)
                xrl[st + 2] = pr.op("sync", (lambda st, b: lambda e: e.dma_start(
                    out=xt[b][:, :], in_=x_hbm[(st + 2) * P:(st + 3) * P, :]))(st, b),
                    waits=[vadd[st]], inc=f"rs{st % 2}", n=16)

        final_waits = [("pout", pr.counts["pout"]), ("pout2", pr.counts["pout2"])]
        if debug:
            for s in ("pv", "pv2", "pq", "pq2", "po", "po2", "pg", "pg2",
                      "px8", "px82", "pw8", "pw82"):
                final_waits.append((s, pr.counts[s]))
        pr.op("sync", lambda e: e.nop(), waits=final_waits)

        @block.sync
        def _(e):
            pr.emit(e, "sync", handles)

        @block.gpsimd
        def _(e):
            pr.emit(e, "gpsimd", handles)

        @block.vector
        def _(e):
            pr.emit(e, "vector", handles)

        @block.scalar
        def _(e):
            pr.emit(e, "scalar", handles)

        @block.tensor
        def _(e):
            pr.emit(e, "tensor", handles)

    return nc


def _layout_inputs(L0_attn_x, L0_attn_Wq, L0_attn_Wk, L0_attn_Wv, L0_attn_Wo,
                   L0_ffn_weight):
    def panelize(w):
        return np.ascontiguousarray(
            np.asarray(w).reshape(NT, P, NT, P).transpose(2, 1, 0, 3).reshape(NT, P, D))

    wqp = panelize(L0_attn_Wq)
    wkp = panelize(L0_attn_Wk)
    wvp = panelize(L0_attn_Wv)
    wo = np.ascontiguousarray(np.asarray(L0_attn_Wo))
    wft = np.ascontiguousarray(np.asarray(L0_ffn_weight).T)
    xs = np.asarray(L0_attn_x)
    return [{"x": np.ascontiguousarray(xs[c]), "wqp": wqp, "wkp": wkp,
             "wvp": wvp, "wo": wo, "wft": wft} for c in range(NCORES)]


def kernel(L0_attn_x, L0_attn_Wq, L0_attn_Wk, L0_attn_Wv, L0_attn_Wo,
           L0_ffn_weight):
    in_maps = _layout_inputs(L0_attn_x, L0_attn_Wq, L0_attn_Wk, L0_attn_Wv,
                             L0_attn_Wo, L0_ffn_weight)
    nc = build_nc(debug=DEBUG)
    res = run_bass_kernel_spmd(nc, in_maps, core_ids=list(range(NCORES)))
    return np.stack([res.results[c]["out"] for c in range(NCORES)], axis=0)


# version 16
# speedup vs baseline: 1.0230x; 1.0186x over previous
import contextlib
import numpy as np
import concourse.bass as bass
import concourse.mybir as mybir
from concourse.bass_utils import run_bass_kernel_spmd

F32 = mybir.dt.float32
BF16 = mybir.dt.bfloat16
FP8 = mybir.dt.float8e4
AF = mybir.ActivationFunctionType
ALU = mybir.AluOpType
AX = mybir.AxisListType

P = 128
D = 2048
S = 2048
NT = D // P
NCORES = 8
E4M3_MAX = 448.0

ENGINES = ("sync", "gpsimd", "vector", "scalar", "tensor")
DMA_SEMS = (["rs0", "rs1", "rs2", "rs3", "rs4", "sv", "svb", "svc", "svd",
             "sw8r", "gc",
             "sq", "sq2", "sg2", "sg22", "so", "so2", "sxr", "sxr2",
             "gw", "gw2", "pv", "pv2", "pq", "pq2", "po", "po2",
             "pg", "pg2", "pout", "pout2", "px8", "px82", "pw8", "pw82"])
ORDER_SEMS = {"vector": "ov", "scalar": "os", "gpsimd": "og"}
SEMS = DMA_SEMS + ["tt", "tg", "ov", "os", "og"]

DEBUG = False


class Prog:

    def __init__(self):
        self.ops = {e: [] for e in ENGINES}
        self.counts = {s: 0 for s in SEMS}

    def op(self, eng, fn, waits=(), inc=None, n=1):
        waits = [w for w in waits if w is not None]
        osem = ORDER_SEMS.get(eng)
        if osem is not None:
            c = self.counts[osem]
            if c > 0:
                waits.append((osem, c))
            if inc is None or inc not in DMA_SEMS:
                self.counts[osem] = c + 1
                self.ops[eng].append((waits, fn, osem, 1))
                return (osem, c + 1)
        self.ops[eng].append((waits, fn, inc, n))
        if inc is not None:
            self.counts[inc] += n
            return (inc, self.counts[inc])
        return None

    def emit(self, eng_obj, eng_name, handles):
        seen = {}
        for waits, fn, inc, n in self.ops[eng_name]:
            for (s, c) in waits:
                if seen.get(s, 0) >= c:
                    continue
                eng_obj.wait_ge(handles[s], c)
                seen[s] = c
            inst = fn(eng_obj)
            if inc is not None:
                inst.then_inc(handles[inc], n)


def build_nc(debug=False):
    nc = bass.Bass("TRN2", target_bir_lowering=False)

    x_hbm = nc.dram_tensor("x", [S, D], F32, kind="ExternalInput").ap()
    wqp = nc.dram_tensor("wqp", [NT, P, D], F32, kind="ExternalInput").ap()
    wkp = nc.dram_tensor("wkp", [NT, P, D], F32, kind="ExternalInput").ap()
    wvp = nc.dram_tensor("wvp", [NT, P, D], F32, kind="ExternalInput").ap()
    wo_hbm = nc.dram_tensor("wo", [D, D], F32, kind="ExternalInput").ap()
    wft = nc.dram_tensor("wft", [D, D], BF16, kind="ExternalInput").ap()
    out_hbm = nc.dram_tensor("out", [S, D], F32, kind="ExternalOutput").ap()

    sk = "ExternalOutput" if debug else "Internal"
    v_sp = nc.dram_tensor("v_sp", [NT, P, S], BF16, kind=sk).ap()
    q_sp = nc.dram_tensor("q_sp", [NT, P, D], BF16, kind=sk).ap()
    o_sp = nc.dram_tensor("o_sp", [NT, P, S], BF16, kind=sk).ap()
    g_sp = nc.dram_tensor("g_sp", [NT, P, D], BF16, kind=sk).ap()
    x8_sp = nc.dram_tensor("x8_sp", [NT, P, D], FP8, kind=sk).ap()
    w8_sp = nc.dram_tensor("w8_sp", [NT, P, D], FP8, kind=sk).ap()
    red_sp = nc.dram_tensor("red_sp", [2, P, 1], F32).ap()
    cc_in = nc.dram_tensor("cc_in", [1, 1], F32).ap()
    cc_out = nc.dram_tensor("cc_out", [1, 1], F32).ap()
    bc_sp = nc.dram_tensor("bc_sp", [1, 4], F32).ap()
    if debug:
        dbg_xn = nc.dram_tensor("dbg_xn", [P, NT * S], BF16, kind="ExternalOutput").ap()
        dbg_st = nc.dram_tensor("dbg_st", [P, 192], F32, kind="ExternalOutput").ap()

    with contextlib.ExitStack() as stack:
        ec = stack.enter_context
        buf1 = ec(nc.sbuf_tensor("buf1", [P, NT * S], BF16))
        buf2 = ec(nc.sbuf_tensor("buf2", [P, NT * D], BF16))
        xtile0 = ec(nc.sbuf_tensor("xtile0", [P, D], F32))
        xtile1 = ec(nc.sbuf_tensor("xtile1", [P, D], F32))
        stage0 = ec(nc.sbuf_tensor("stage0", [P, D], BF16))
        stage1 = ec(nc.sbuf_tensor("stage1", [P, D], BF16))
        wpan0 = ec(nc.sbuf_tensor("wpan0", [P, D], BF16))
        wpan1 = ec(nc.sbuf_tensor("wpan1", [P, D], BF16))
        evac0 = ec(nc.sbuf_tensor("evac0", [P, D], BF16))
        evac1 = ec(nc.sbuf_tensor("evac1", [P, D], BF16))
        escr = ec(nc.sbuf_tensor("escr", [P, D], F32))
        ptm = ec(nc.sbuf_tensor("ptm", [P, D], BF16))
        pfm = ec(nc.sbuf_tensor("pfm", [P, D], BF16))
        outt0 = ec(nc.sbuf_tensor("outt0", [P, D], F32))
        outt1 = ec(nc.sbuf_tensor("outt1", [P, D], F32))
        stats = ec(nc.sbuf_tensor("stats", [P, 192], F32))
        ident = ec(nc.sbuf_tensor("ident", [P, P], BF16))
        redrow = ec(nc.sbuf_tensor("redrow", [1, 2 * P], F32))
        psA = ec(nc.psum_tensor("psA", [P, 2048], F32))
        psB = ec(nc.psum_tensor("psB", [P, 2048], F32))
        handles = {s: ec(nc.semaphore(s)) for s in SEMS}
        nc.all_engine_barrier()
        lo = min(h.num for h in handles.values())
        nc.gpsimd.sem_clear(range(lo, bass.get_kernel_semaphore_range().stop))
        nc.all_engine_barrier()
        block = ec(nc.Block())
        pr = Prog()

        xt = [xtile0, xtile1]
        stg = [stage0, stage1]
        wpn = [wpan0, wpan1]
        evc = [evac0, evac1]
        ott = [outt0, outt1]
        pp = [ptm, pfm]
        ps = [psA, psB]
        psbit = [psA.ap().bitcast(BF16), psB.ap().bitcast(BF16)]
        evc8 = [evac0.ap().bitcast(FP8), evac1.ap().bitcast(FP8)]
        stg8 = [stage0.ap().bitcast(FP8), stage1.ap().bitcast(FP8)]
        escr_b = escr.ap().bitcast(BF16)
        buf1f8 = buf1.ap().bitcast(FP8)
        w8v = buf1f8[:, NT * S:2 * NT * S]
        ring = [xtile0, xtile1, outt0, outt1, escr]
        RD = len(ring)

        c_max = stats[:, 0:NT]
        c_min = stats[:, NT:2 * NT]
        c_sum = stats[:, 2 * NT:3 * NT]
        c_ssq = stats[:, 3 * NT:4 * NT]
        c_rstd = stats[:, 4 * NT:5 * NT]
        c_nmr = stats[:, 5 * NT:6 * NT]
        c_amax = stats[:, 6 * NT:7 * NT]
        c_qs = stats[:, 7 * NT:8 * NT]
        c_mean = stats[:, 8 * NT:9 * NT]
        c_qb = stats[:, 9 * NT:10 * NT]
        k0 = 10 * NT
        c_t0 = stats[:, k0:k0 + 1]
        c_recip = stats[:, k0 + 3:k0 + 4]
        c_bcx = stats[:, k0 + 4:k0 + 5]
        c_bcw = stats[:, k0 + 5:k0 + 6]
        c_bcg = stats[:, k0 + 6:k0 + 7]
        c_bc3 = stats[:, k0 + 4:k0 + 7]
        c_am1 = stats[:, k0 + 8:k0 + 9]
        c_wam1 = stats[:, k0 + 9:k0 + 10]
        c_wamp = stats[:, k0 + 16:k0 + 16 + NT]

        ev = {}

        def gsem(i):
            return "gw" if i % 2 == 0 else "gw2"

        def s2(base, i):
            return base if i % 2 == 0 else base + "2"

        pr.op("gpsimd", lambda g: g.memset(ident[:, :], 0.0))
        ev["ident"] = pr.op(
            "gpsimd",
            lambda g: g.affine_select(out=ident[:, :], in_=ident[:, :],
                                      compare_op=ALU.not_equal, fill=1.0, base=0,
                                      pattern=[[-1, P]], channel_multiplier=1),
            inc="tg")

        pan = {}
        for ft in range(2):
            pan[ft] = pr.op("gpsimd", (lambda ft: lambda e: e.dma_start(
                out=wpn[ft % 2][:, :], in_=wvp[ft]))(ft),
                waits=[], inc=gsem(ft), n=16)
        wfld = [None] * NT
        for c in range(NT):
            wfld[c] = pr.op("gpsimd", (lambda c: lambda e: e.dma_start(
                out=buf2[:, c * D:(c + 1) * D], in_=wft[c * P:(c + 1) * P, :]))(c),
                waits=[], inc=gsem(c), n=16)


        ring5 = [xtile0, xtile1, outt0, outt1, escr]
        vdone = [None] * NT
        sdone = [None] * NT
        apl = [None] * NT
        xn_tr = [None] * NT
        xnev = [None] * NT
        nmr_g = [None] * 4
        inv_d = float(np.float32(1.0 / D))
        used = {}

        def emit_c_group(ft, ns, extra):
            h = ft % 2
            mm = None
            for c in range(NT):
                w = list(extra) if c == 0 else []
                mm = pr.op("tensor", (lambda ft, ns, c, h: lambda e: e.matmul(
                    ps[h][:, ns * 512:(ns + 1) * 512],
                    wpn[ft % 2][:, c * P:(c + 1) * P],
                    buf1[:, c * S + ns * 512:c * S + (ns + 1) * 512],
                    start=(c == 0), stop=(c == NT - 1)))(ft, ns, c, h),
                    waits=w if c == 0 else (),
                    inc=("tt" if c == NT - 1 else None))
            return mm

        def emit_xnev(i):
            if i < 0 or xnev[i] is not None:
                return
            xnev[i] = pr.op("vector", (lambda i: lambda e: e.tensor_copy(
                buf1[:, :].rearrange("p (c s) -> p c s", c=NT)[:, :, i * P:(i + 1) * P],
                psbit[1][:, 0:D].rearrange("p (c t) -> p c t", c=NT)))(i),
                waits=[xn_tr[i]])

        def ln_chain(g):
            cs = slice(4 * g, 4 * g + 4)
            pr.op("vector", (lambda cs: lambda e: e.tensor_scalar_mul(
                c_mean[:, cs], c_sum[:, cs], inv_d))(cs),
                waits=[sdone[4 * g + 3]])
            pr.op("vector", (lambda cs: lambda e: e.tensor_mul(
                c_qb[:, cs], c_mean[:, cs], c_mean[:, cs]))(cs))
            pr.op("vector", (lambda cs: lambda e: e.tensor_scalar_mul(
                c_rstd[:, cs], c_ssq[:, cs], inv_d))(cs))
            pr.op("vector", (lambda cs: lambda e: e.tensor_sub(
                c_rstd[:, cs], c_rstd[:, cs], c_qb[:, cs]))(cs))
            var_ev = pr.op("vector", (lambda cs: lambda e: e.tensor_scalar_add(
                c_rstd[:, cs], c_rstd[:, cs], 1e-5))(cs))
            s_ev = pr.op("scalar", (lambda cs: lambda e: e.activation(
                c_qs[:, cs], c_rstd[:, cs], AF.Sqrt))(cs), waits=[var_ev])
            last = pr.op("vector", (lambda cs: lambda e: e.reciprocal(
                c_nmr[:, cs], c_qs[:, cs]))(cs), waits=[s_ev])
            for _ in range(2):
                pr.op("vector", (lambda cs: lambda e: e.tensor_mul(
                    c_qb[:, cs], c_nmr[:, cs], c_nmr[:, cs]))(cs), waits=[last])
                m2 = pr.op("vector", (lambda cs: lambda e: e.tensor_mul(
                    c_qb[:, cs], c_qb[:, cs], c_rstd[:, cs]))(cs))
                t_ev = pr.op("scalar", (lambda cs: lambda e: e.activation(
                    c_qb[:, cs], c_qb[:, cs], AF.Copy, scale=-0.5, bias=1.5))(cs),
                    waits=[m2])
                last = pr.op("vector", (lambda cs: lambda e: e.tensor_mul(
                    c_nmr[:, cs], c_nmr[:, cs], c_qb[:, cs]))(cs), waits=[t_ev])
            pr.op("vector", (lambda cs: lambda e: e.tensor_copy(
                c_rstd[:, cs], c_nmr[:, cs]))(cs), waits=[last])
            pr.op("vector", (lambda cs: lambda e: e.tensor_mul(
                c_nmr[:, cs], c_mean[:, cs], c_rstd[:, cs]))(cs))
            nmr_g[g] = pr.op("vector", (lambda cs: lambda e: e.tensor_scalar_mul(
                c_nmr[:, cs], c_nmr[:, cs], -1.0))(cs))
            pr.op("vector", (lambda cs: lambda e: e.tensor_sub(
                c_amax[:, cs], c_max[:, cs], c_mean[:, cs]))(cs))
            pr.op("vector", (lambda cs: lambda e: e.tensor_sub(
                c_qs[:, cs], c_mean[:, cs], c_min[:, cs]))(cs))
            pr.op("vector", (lambda cs: lambda e: e.tensor_max(
                c_amax[:, cs], c_amax[:, cs], c_qs[:, cs]))(cs))
            return pr.op("vector", (lambda cs: lambda e: e.tensor_mul(
                c_amax[:, cs], c_amax[:, cs], c_rstd[:, cs]))(cs))

        amx_ev = None
        for g in range(4):
            for i in range(4 * g, 4 * g + 4):
                b = i % 5
                w = [apl[i - 5]] if i >= 5 else []
                ld = pr.op("sync", (lambda i, b: lambda e: e.dma_start(
                    out=ring5[b][:, :], in_=x_hbm[i * P:(i + 1) * P, :]))(i, b),
                    waits=w, inc=f"rs{i % 2}", n=16)
                pr.op("vector", (lambda i, b: lambda e: e.reduce_max(
                    c_max[:, i:i + 1], ring5[b][:, :], axis=AX.X))(i, b),
                    waits=[ld])
                vdone[i] = pr.op("vector", (lambda i, b: lambda e: e.tensor_reduce(
                    c_min[:, i:i + 1], ring5[b][:, :], axis=AX.X, op=ALU.min))(i, b))
                pr.op("scalar", (lambda i, b: lambda e: e.activation(
                    ptm[:, :], ring5[b][:, :], AF.Copy,
                    accum_out=c_sum[:, i:i + 1]))(i, b), waits=[ld])
                sdone[i] = pr.op("scalar", (lambda i, b: lambda e: e.activation(
                    ptm[:, :], ring5[b][:, :], AF.Square,
                    accum_out=c_ssq[:, i:i + 1]))(i, b))
            amx_ev = ln_chain(g)
            if g == 3:
                am_ev = pr.op("vector", lambda e: e.reduce_max(
                    c_am1, c_amax, axis=AX.X), waits=[amx_ev])
                d1 = pr.op("gpsimd", lambda e: e.dma_start(
                    out=red_sp[0], in_=c_am1), waits=[am_ev], inc="gc", n=16)
                d3a = pr.op("gpsimd", lambda e: e.dma_start(
                    out=redrow[0:1, 0:P], in_=red_sp[0].rearrange("p o -> o p")),
                    waits=[d1], inc="gc", n=16)
            for i in range(4 * g, 4 * g + 4):
                b = i % 5
                aw = [nmr_g[g]] + ([xn_tr[i - 2]] if i >= 2 else [])
                apl[i] = pr.op("scalar", (lambda i, b: lambda e: e.activation(
                    stg[i % 2][:, :], ring5[b][:, :], AF.Identity,
                    scale=c_rstd[:, i:i + 1], bias=c_nmr[:, i:i + 1]))(i, b),
                    waits=aw)
                emit_xnev(i - 1)
                tw = [apl[i], ev["ident"]] + ([xnev[i - 1]] if i >= 1 else [])
                tr = None
                for c in range(NT):
                    tr = pr.op("tensor", (lambda i, c: lambda e: e.transpose(
                        psbit[1][:, c * P:(c + 1) * P],
                        stg[i % 2][:, c * P:(c + 1) * P], ident[:, :]))(i, c),
                        waits=tw if c == 0 else (),
                        inc=("tt" if c == NT - 1 else None))
                xn_tr[i] = tr
            emit_xnev(4 * g + 3)
            used[0] = emit_c_group(0, g, [pan[0], xnev[4 * g + 3]])
        nmr_ev = nmr_g[3]
        if debug:
            pr.op("sync", lambda e: e.dma_start(out=dbg_st, in_=stats[:, :]),
                  waits=[am_ev, nmr_ev], inc="sw8r", n=16)

        xam2 = pr.op("vector", lambda e: e.reduce_max(
            stats[0:1, k0:k0 + 1], redrow[0:1, 0:P], axis=AX.X), waits=[d3a])
        wfred = [None] * NT
        for c in range(NT):
            wfred[c] = pr.op("vector", (lambda c: lambda e: e.reduce_max(
                c_wamp[:, c:c + 1], buf2[:, c * D:(c + 1) * D], axis=AX.X,
                apply_absolute_value=True))(c), waits=[wfld[c]])
        d4 = pr.op("gpsimd", lambda e: e.dma_start(
            out=cc_in, in_=stats[0:1, k0:k0 + 1]), waits=[xam2], inc="gc", n=16)
        cc_ev = pr.op("gpsimd", lambda e: e.collective_compute(
            "AllReduce", ALU.max, replica_groups=[list(range(NCORES))],
            ins=[cc_in], outs=[cc_out]), waits=[d4], inc="gc", n=1)
        d5 = pr.op("gpsimd", lambda e: e.dma_start(
            out=stats[0:1, k0:k0 + 1], in_=cc_out), waits=[cc_ev], inc="gc", n=16)
        inv_e4 = float(np.float32(1.0) / np.float32(E4M3_MAX))
        pr.op("vector", lambda e: e.tensor_scalar_mul(
            stats[0:1, k0:k0 + 1], stats[0:1, k0:k0 + 1], inv_e4), waits=[d5])
        pr.op("vector", lambda e: e.tensor_scalar_max(
            stats[0:1, k0:k0 + 1], stats[0:1, k0:k0 + 1], 1e-12))
        pr.op("vector", lambda e: e.tensor_scalar_mul(
            stats[0:1, k0 + 12:k0 + 13], stats[0:1, k0:k0 + 1], 2.0))
        sc_ev = pr.op("vector", lambda e: e.reciprocal(
            stats[0:1, k0 + 12:k0 + 13], stats[0:1, k0 + 12:k0 + 13]))
        d6 = pr.op("gpsimd", lambda e: e.dma_start(
            out=bc_sp[0:1, 0:1], in_=stats[0:1, k0 + 12:k0 + 13]),
            waits=[sc_ev], inc="gc", n=16)
        bc_ev = pr.op("gpsimd", lambda e: e.dma_start(
            out=c_bcx, in_=bc_sp[0:1, 0:1].broadcast_to([P, 1])),
            waits=[d6], inc="gc", n=16)
        pan[2] = pr.op("gpsimd", lambda e: e.dma_start(
            out=wpn[0][:, :], in_=wvp[2]), waits=[used[0]], inc=gsem(2), n=16)
        wamr = pr.op("vector", lambda e: e.reduce_max(
            c_wam1, c_wamp, axis=AX.X), waits=[wfred[NT - 1]])
        d2 = pr.op("gpsimd", lambda e: e.dma_start(out=red_sp[1], in_=c_wam1),
                   waits=[wamr], inc="gc", n=16)
        d3b = pr.op("gpsimd", lambda e: e.dma_start(
            out=redrow[0:1, P:2 * P], in_=red_sp[1].rearrange("p o -> o p")),
            waits=[d2], inc="gc", n=16)
        pr.op("vector", lambda e: e.reduce_max(
            stats[0:1, k0 + 1:k0 + 2], redrow[0:1, P:2 * P], axis=AX.X),
            waits=[d3b])
        pr.op("vector", lambda e: e.tensor_scalar_mul(
            stats[0:1, k0 + 1:k0 + 2], stats[0:1, k0 + 1:k0 + 2], inv_e4))
        pr.op("vector", lambda e: e.tensor_scalar_max(
            stats[0:1, k0 + 1:k0 + 2], stats[0:1, k0 + 1:k0 + 2], 1e-12))
        pr.op("vector", lambda e: e.tensor_mul(
            stats[0:1, k0 + 14:k0 + 15], stats[0:1, k0:k0 + 1],
            stats[0:1, k0 + 1:k0 + 2]))
        pr.op("vector", lambda e: e.tensor_scalar_mul(
            stats[0:1, k0 + 14:k0 + 15], stats[0:1, k0 + 14:k0 + 15], 4.0))
        pr.op("vector", lambda e: e.tensor_scalar_mul(
            stats[0:1, k0 + 13:k0 + 14], stats[0:1, k0 + 1:k0 + 2], 2.0))
        swc = pr.op("vector", lambda e: e.reciprocal(
            stats[0:1, k0 + 13:k0 + 14], stats[0:1, k0 + 13:k0 + 14]))
        d6w = pr.op("gpsimd", lambda e: e.dma_start(
            out=bc_sp[0:1, 1:3], in_=stats[0:1, k0 + 13:k0 + 15]),
            waits=[swc], inc="gc", n=16)
        ev["bcw"] = pr.op("gpsimd", lambda e: e.dma_start(
            out=stats[:, k0 + 5:k0 + 7],
            in_=bc_sp[0:1, 1:3].broadcast_to([P, 2])),
            waits=[d6w], inc="gc", n=16)

        ev["xn_done"] = xnev[NT - 1]
        if debug:
            pr.op("sync", lambda e: e.dma_start(out=dbg_xn, in_=buf1[:, :]),
                  waits=[ev["xn_done"]], inc="sw8r", n=16)

        freeH = {0: None, 1: None}
        vsp = [None] * NT
        eva0 = pr.op("scalar", lambda e: e.activation(
            evc[0][:, :], ps[0][:, :], AF.Copy), waits=[used[0]])
        freeH[0] = eva0
        vsp[0] = pr.op("scalar", lambda e: e.dma_start(
            out=v_sp[0], in_=evc[0][:, :]), waits=[eva0], inc="pv", n=16)

        wftre = [None] * NT
        w8q = [None] * NT
        w8spv = [None] * NT
        x8q = [None] * NT
        x8sp = [None] * NT
        ob8 = [outt0.ap().bitcast(FP8), outt1.ap().bitcast(FP8)]

        def emit_w8_band(c):
            b = c % 2
            w = [sdone[NT - 1]] if c < 2 else [w8q[c - 2]]
            wftre[c] = pr.op("gpsimd", (lambda c, b: lambda e: e.dma_start(
                out=pp[b][:, :], in_=wft[c * P:(c + 1) * P, :]))(c, b),
                waits=w, inc=gsem(c), n=16)
            qw = [wftre[c], ev["bcw"]]
            if c < 2:
                qw.append(xn_tr[NT - 2 + c])
            else:
                qw.append(w8spv[c - 2])
            w8q[c] = pr.op("vector", (lambda c, b: lambda e: e.tensor_scalar(
                stg8[b][:, 0:D], pp[b][:, :], c_bcw, None, op0=ALU.mult))(c, b),
                waits=qw)
            w8spv[c] = pr.op("scalar", (lambda c, b: lambda e: e.dma_start(
                out=w8_sp[c], in_=stg8[b][:, 0:D]))(c, b),
                waits=[w8q[c]], inc=s2("pw8", c), n=16)

        def emit_x8_tile(i):
            b = i % 2
            qw = [xnev[i], bc_ev]
            qw.append(apl[12 + b] if i < 2 else x8sp[i - 2])
            x8q[i] = pr.op("vector", (lambda i, b: lambda e: e.tensor_scalar(
                ob8[b][:, 0:D].rearrange("p (c t) -> p c t", c=NT),
                buf1[:, :].rearrange("p (c s) -> p c s", c=NT)[:, :, i * P:(i + 1) * P],
                c_bcx, None, op0=ALU.mult))(i, b), waits=qw)
            x8sp[i] = pr.op("scalar", (lambda i, b: lambda e: e.dma_start(
                out=x8_sp[i], in_=ob8[b][:, 0:D]))(i, b),
                waits=[x8q[i]], inc=s2("px8", i), n=16)

        for ft in range(1, NT):
            h = ft % 2
            extra = [pan[ft]]
            if ft == 1:
                extra += [xnev[NT - 1]]
            if freeH[h] is not None:
                extra.append(freeH[h])
            mm = None
            for ns in range(4):
                mm = emit_c_group(ft, ns, extra if ns == 0 else [])
            used[ft] = mm
            if ft + 2 < NT:
                pan[ft + 2] = pr.op("gpsimd", (lambda ft: lambda e: e.dma_start(
                    out=wpn[ft % 2][:, :], in_=wvp[ft + 2]))(ft),
                    waits=[mm], inc=gsem(ft), n=16)
            evw = [mm] + ([vsp[ft - 2]] if ft >= 2 else [])
            eva = pr.op("scalar", (lambda ft, h: lambda e: e.activation(
                evc[ft % 2][:, :], ps[h][:, :], AF.Copy))(ft, h), waits=evw)
            freeH[h] = eva
            vsp[ft] = pr.op("scalar", (lambda ft: lambda e: e.dma_start(
                out=v_sp[ft], in_=evc[ft % 2][:, :]))(ft), waits=[eva],
                inc=s2("pv", ft), n=16)
            emit_w8_band(ft - 1)
            emit_x8_tile(ft - 1)
        emit_w8_band(NT - 1)
        emit_x8_tile(NT - 1)

        def proj_band(ft, pan_ev, pan_buf, extra=(), mid=None):
            h = ft % 2
            mm = None
            for c in range(NT):
                w = [pan_ev, freeH[h]] + list(extra) if c == 0 else []
                for nsl in range(4):
                    mm = pr.op("tensor", (lambda c, nsl, h, pan_buf: lambda e: e.matmul(
                        ps[h][:, nsl * 512:(nsl + 1) * 512],
                        pan_buf[:, c * P:(c + 1) * P],
                        buf1[:, c * S + nsl * 512:c * S + (nsl + 1) * 512],
                        start=(c == 0), stop=(c == NT - 1)))(c, nsl, h, pan_buf),
                        waits=w if nsl == 0 else (),
                        inc=("tt" if (c == NT - 1 and nsl == 3) else None))
                if c == 2 and mid is not None:
                    mid()
            return mm

        qpan = {}
        qused = {}
        qsp = [None] * NT
        qtr = [None] * NT
        for ft in range(2):
            qpan[ft] = pr.op("gpsimd", (lambda ft: lambda e: e.dma_start(
                out=wpn[ft % 2][:, :], in_=wqp[ft]))(ft),
                waits=[used[NT - 2 + ft]], inc=gsem(ft), n=16)
        def d_bev(ft):
            h = ft % 2
            bw = [qused[ft]] + ([qtr[ft - 2]] if ft >= 2
                                else [w8spv[NT - 2 + ft]])
            return pr.op("scalar", (lambda ft, h: lambda e: e.activation(
                stg[ft % 2][:, :], ps[h][:, :], AF.Copy))(ft, h), waits=bw)

        def d_tail(ft, bev):
            h = ft % 2
            tr = None
            for c in range(NT):
                tr = pr.op("tensor", (lambda ft, c, h: lambda e: e.transpose(
                    psbit[h][:, c * P:(c + 1) * P],
                    stg[ft % 2][:, c * P:(c + 1) * P], ident[:, :]))(ft, c, h),
                    waits=[bev] if c == 0 else (),
                    inc=("tt" if c == NT - 1 else None))
            qtr[ft] = tr
            qe = pr.op("vector", (lambda ft, h: lambda e: e.tensor_copy(
                evc[ft % 2][:, :], psbit[h][:, 0:D]))(ft, h),
                waits=[tr] + ([qsp[ft - 2]] if ft >= 2 else [vsp[NT - 2 + ft]]))
            freeH[h] = qe
            qsp[ft] = pr.op("scalar", (lambda ft: lambda e: e.dma_start(
                out=q_sp[ft], in_=evc[ft % 2][:, :]))(ft), waits=[qe],
                inc=s2("pq", ft), n=16)

        dbev = {}
        for ft in range(NT):
            h = ft % 2
            mid = (lambda b=ft - 1: d_tail(b, dbev[b])) if ft >= 1 else None
            mm = proj_band(ft, qpan[ft], wpn[ft % 2], mid=mid)
            qused[ft] = mm
            dbev[ft] = d_bev(ft)
            if ft + 2 < NT:
                qpan[ft + 2] = pr.op("gpsimd", (lambda ft: lambda e: e.dma_start(
                    out=wpn[ft % 2][:, :], in_=wqp[ft + 2]))(ft),
                    waits=[mm], inc=gsem(ft), n=16)
        d_tail(NT - 1, dbev[NT - 1])

        kpan = {}
        kused = {}
        ktr = [None] * NT
        ktrv = [None] * NT
        for ft in range(2):
            kpan[ft] = pr.op("gpsimd", (lambda ft: lambda e: e.dma_start(
                out=wpn[ft % 2][:, :], in_=wkp[ft]))(ft),
                waits=[qused[NT - 2 + ft]], inc=gsem(ft), n=16)
        def e_bev(ft):
            h = ft % 2
            bw = [kused[ft]] + ([ktr[ft - 2]] if ft >= 2
                                else [qtr[NT - 2 + ft]])
            return pr.op("scalar", (lambda ft, h: lambda e: e.activation(
                stg[ft % 2][:, :], ps[h][:, :], AF.Copy))(ft, h), waits=bw)

        def e_tail(ft, bev):
            h = ft % 2
            tr = None
            for c in range(NT):
                tr = pr.op("tensor", (lambda ft, c, h: lambda e: e.transpose(
                    psbit[h][:, c * P:(c + 1) * P],
                    stg[ft % 2][:, c * P:(c + 1) * P], ident[:, :]))(ft, c, h),
                    waits=[bev] if c == 0 else (),
                    inc=("tt" if c == NT - 1 else None))
            ktr[ft] = tr
            ktrv[ft] = pr.op("vector", (lambda ft, h: lambda e: e.tensor_copy(
                buf2[:, :].rearrange("p (j f) -> p j f", j=NT)[:, :, ft * P:(ft + 1) * P],
                psbit[h][:, 0:D].rearrange("p (j t) -> p j t", j=NT)))(ft, h),
                waits=[tr] + ([qsp[NT - 2 + ft]] if ft < 2 else []))
            freeH[h] = ktrv[ft]

        ebev = {}
        for ft in range(NT):
            h = ft % 2
            mid = (lambda b=ft - 1: e_tail(b, ebev[b])) if ft >= 1 else None
            mm = proj_band(ft, kpan[ft], wpn[ft % 2], mid=mid)
            kused[ft] = mm
            ebev[ft] = e_bev(ft)
            if ft + 2 < NT:
                kpan[ft + 2] = pr.op("gpsimd", (lambda ft: lambda e: e.dma_start(
                    out=wpn[ft % 2][:, :], in_=wkp[ft + 2]))(ft),
                    waits=[mm], inc=gsem(ft), n=16)
        e_tail(NT - 1, ebev[NT - 1])
        ev["k_done"] = ktrv[NT - 1]

        st8 = [None] * 6
        for c in range(6):
            st8[c] = pr.op("gpsimd", (lambda c: lambda e: e.dma_start(
                out=ob8[c % 2][:, D + (c // 2) * D:D + (c // 2 + 1) * D],
                in_=w8_sp[c]))(c),
                waits=[w8spv[c], x8sp[NT - 2], x8sp[NT - 1]], inc="gc", n=16)

        qrl = [None] * NT
        qrl[0] = pr.op("sync", lambda e: e.dma_start(
            out=ptm[:, :], in_=q_sp[0]),
            waits=[qsp[0], w8q[NT - 2]], inc="sq", n=16)
        qrl[1] = pr.op("sync", lambda e: e.dma_start(
            out=wpn[1][:, :], in_=q_sp[1]),
            waits=[qsp[1], kused[NT - 1]], inc="sq2", n=16)

        vgrp = [None] * 4
        vg_sems = ["sv", "svb", "svc", "svd"]
        vg_eng = ["sync", "scalar", "gpsimd", "sync"]
        for k in range(4):
            vgrp[k] = pr.op(vg_eng[k], (lambda k: lambda e: e.dma_start(
                out=buf1[:, 4 * k * S:(4 * k + 4) * S].rearrange(
                    "p (c s) -> p c s", c=4),
                in_=v_sp[4 * k:4 * k + 4].rearrange("c p s -> p c s")))(k),
                waits=[vsp[4 * k + 3], vsp[4 * k + 2], kused[NT - 1],
                       x8q[NT - 1]],
                inc=vg_sems[k], n=16)
        ev["v_loaded"] = vgrp[3]

        def q_buf(et):
            return ptm if et == 0 else wpn[et % 2]

        def emit_scores(et, extra):
            h = et % 2
            mm = None
            for j in range(NT):
                w = [qrl[et], ev["k_done"]] + list(extra) if j == 0 else []
                for nsl in range(4):
                    mm = pr.op("tensor", (lambda et, j, nsl, h: lambda e: e.matmul(
                        ps[h][:, nsl * 512:(nsl + 1) * 512],
                        q_buf(et)[:, j * P:(j + 1) * P],
                        buf2[:, j * D + nsl * 512:j * D + (nsl + 1) * 512],
                        start=(j == 0), stop=(j == NT - 1)))(et, j, nsl, h),
                        waits=w if nsl == 0 else (),
                        inc=("tt" if (j == NT - 1 and nsl == 3) else None))
            return mm

        sc_done = [None] * NT
        exp_done = [None] * NT
        osp = [None] * NT
        oev = [None] * NT
        o_done = [None] * NT

        sc_done[0] = emit_scores(0, [freeH[0]])
        sc_done[1] = emit_scores(1, [freeH[1]])

        for et in range(NT):
            h = et % 2
            b = et % 2
            vm = pr.op("vector", (lambda h: lambda e: e.reduce_max(
                c_t0, ps[h][:, :], axis=AX.X, negate=True))(h),
                waits=[sc_done[et]])
            vb = pr.op("vector", lambda e: e.tensor_scalar_mul(c_t0, c_t0, 0.125))
            exp_done[et] = pr.op("scalar", (lambda h: lambda e: e.activation(
                escr[:, :], ps[h][:, :], AF.Exp, bias=c_t0, scale=0.125,
                accum_out=c_recip))(h), waits=[vb])
            rc = pr.op("vector", lambda e: e.reciprocal(c_recip, c_recip),
                       waits=[exp_done[et]])
            ptm_ev = pr.op("vector", lambda e: e.tensor_copy(ptm[:, :], escr[:, :]),
                           waits=[exp_done[et]])
            tr = None
            for c in range(NT):
                tr = pr.op("tensor", (lambda c, h: lambda e: e.transpose(
                    psbit[h][:, c * P:(c + 1) * P],
                    ptm[:, c * P:(c + 1) * P], ident[:, :]))(c, h),
                    waits=[ptm_ev] if c == 0 else (),
                    inc=("tt" if c == NT - 1 else None))
            pfm_ev = pr.op("vector", (lambda h: lambda e: e.tensor_copy(
                pfm[:, :], psbit[h][:, 0:D]))(h), waits=[tr])
            mm2 = None
            for c in range(NT):
                w = []
                if c == 0:
                    w = [pfm_ev, vgrp[0]]
                elif c % 4 == 0:
                    w = [vgrp[c // 4]]
                for nsl in range(4):
                    mm2 = pr.op("tensor", (lambda c, nsl, h: lambda e: e.matmul(
                        ps[h][:, nsl * 512:(nsl + 1) * 512],
                        pfm[:, c * P:(c + 1) * P],
                        buf1[:, c * S + nsl * 512:c * S + (nsl + 1) * 512],
                        start=(c == 0), stop=(c == NT - 1)))(c, nsl, h),
                        waits=w if nsl == 0 else (),
                        inc=("tt" if (c == NT - 1 and nsl == 3) else None))
            o_done[et] = mm2
            oev[et] = pr.op("scalar", (lambda b, h: lambda e: e.activation(
                evc[b][:, :], ps[h][:, :], AF.Copy, scale=c_recip))(b, h),
                waits=[rc, mm2] + ([osp[et - 2]] if et >= 2 else [qsp[NT - 2 + et]]))
            osp[et] = pr.op("scalar", (lambda et, b: lambda e: e.dma_start(
                out=o_sp[:, :, et * P:(et + 1) * P].rearrange("j p s -> p j s"),
                in_=evc[b][:, :].rearrange("p (j s) -> p j s", j=NT)))(et, b),
                inc=s2("po", et), n=16)
            if et + 2 < NT:
                qrl[et + 2] = pr.op("sync", (lambda et: lambda e: e.dma_start(
                    out=wpn[et % 2][:, :], in_=q_sp[et + 2]))(et),
                    waits=[qsp[et + 2], sc_done[et]], inc=s2("sq", et), n=16)
                freeH[et % 2] = oev[et]
                sc_done[et + 2] = emit_scores(et + 2, [freeH[et % 2]])

        w8rA = pr.op("sync", lambda e: e.dma_start(
            out=w8v[:, 6 * D:11 * D].rearrange("p (c j) -> p c j", c=5),
            in_=w8_sp[6:11].rearrange("c p j -> p c j")),
            waits=[o_done[NT - 1], w8spv[NT - 2], w8spv[NT - 1]], inc="sw8r", n=16)
        w8rB = pr.op("gpsimd", lambda e: e.dma_start(
            out=w8v[:, 11 * D:16 * D].rearrange("p (c j) -> p c j", c=5),
            in_=w8_sp[11:16].rearrange("c p j -> p c j")),
            waits=[o_done[NT - 1], w8spv[NT - 2], w8spv[NT - 1]], inc="gc", n=16)
        x8rld = [None] * NT
        ffn_mm = [None] * NT
        ffn_free = [None] * NT
        gsp = [None] * NT
        for i in range(2):
            x8rld[i] = pr.op("sync", (lambda i: lambda e: e.dma_start(
                out=evc8[i % 2][:, 0:D], in_=x8_sp[i]))(i),
                waits=[osp[NT - 2 + i], x8sp[i]],
                inc=("sxr" if i % 2 == 0 else "sxr2"), n=16)
        wo_ev = [None] * NT
        for c in range(NT):
            w = [sc_done[NT - 1]] if c < 2 else [wo_ev[c - 2]]
            wo_ev[c] = pr.op("gpsimd", (lambda c: lambda e: e.dma_start(
                out=buf2[:, c * D:(c + 1) * D], in_=wo_hbm[c * P:(c + 1) * P, :]))(c),
                waits=w, inc=gsem(c), n=16)
        ev["wo_done"] = wo_ev[NT - 1]
        ev["wo_done2"] = wo_ev[NT - 2]

        def w8rhs(c, nsl):
            if c < 6:
                return ob8[c % 2][:, D + (c // 2) * D + nsl * 512:
                                  D + (c // 2) * D + (nsl + 1) * 512]
            return w8v[:, c * D + nsl * 512:c * D + (nsl + 1) * 512]

        for i in range(NT):
            h = i % 2
            mm = None
            for c in range(NT):
                w = []
                if c == 0:
                    w = [x8rld[i]]
                    w.append(oev[NT - 2 + i] if i < 2 else ffn_free[i - 2])
                if i == 0:
                    if c == 0:
                        w += [st8[5]]
                    elif c == 6:
                        w += [w8rA]
                    elif c == 11:
                        w += [w8rB]
                for nsl in range(4):
                    mm = pr.op("tensor", (lambda i, c, nsl, h: lambda e: e.matmul(
                        ps[h][:, nsl * 512:(nsl + 1) * 512],
                        evc8[i % 2][:, c * P:(c + 1) * P],
                        w8rhs(c, nsl),
                        start=(c == 0), stop=(c == NT - 1)))(i, c, nsl, h),
                        waits=w if nsl == 0 else (),
                        inc=("tt" if (c == NT - 1 and nsl == 3) else None))
            ffn_mm[i] = mm
            if i + 2 < NT:
                x8rld[i + 2] = pr.op("sync", (lambda i: lambda e: e.dma_start(
                    out=evc8[i % 2][:, 0:D], in_=x8_sp[i + 2]))(i),
                    waits=[mm, x8sp[i + 2]],
                    inc=("sxr" if i % 2 == 0 else "sxr2"), n=16)
            o1 = pr.op("scalar", (lambda h: lambda e: e.activation(
                escr_b[:, 0:D], ps[h][:, :], AF.Copy, scale=c_bcg))(h),
                waits=[mm, ev["bcw"]])
            ffn_free[i] = o1
            sg = pr.op("scalar", lambda e: e.activation(
                escr_b[:, D:2 * D], escr_b[:, 0:D], AF.Sigmoid))
            gm = pr.op("vector", (lambda i: lambda e: e.tensor_mul(
                stg[i % 2][:, :], escr_b[:, 0:D], escr_b[:, D:2 * D]))(i),
                waits=[sg] + ([gsp[i - 2]] if i >= 2 else []))
            gsp[i] = pr.op("scalar", (lambda i: lambda e: e.dma_start(
                out=g_sp[i], in_=stg[i % 2][:, :]))(i), waits=[gm],
                inc=s2("pg", i), n=16)

        orl = [None] * NT
        grl = [None] * NT
        xrl = [None] * NT
        for st in range(2):
            orl[st] = pr.op("sync", (lambda st: lambda e: e.dma_start(
                out=wpn[st % 2][:, :], in_=o_sp[st]))(st),
                waits=[osp[NT - 1], sc_done[NT - 1]], inc=s2("so", st), n=16)
            grl[st] = pr.op("sync", (lambda st: lambda e: e.dma_start(
                out=evc[st % 2][:, :], in_=g_sp[st]))(st),
                waits=[gsp[st], ffn_mm[NT - 2 + st]], inc=s2("sg2", st), n=16)
            xg = 15 if st == 0 else 11
            xrl[st] = pr.op("sync", (lambda st: lambda e: e.dma_start(
                out=xt[st % 2][:, :], in_=x_hbm[st * P:(st + 1) * P, :]))(st),
                waits=[apl[xg]], inc=f"rs{st % 2}", n=16)
        out_ev = [None] * NT
        vadd = [None] * NT
        wo_free = [None] * NT
        for st in range(NT):
            h = st % 2
            b = st % 2
            mm = None
            for c in range(NT):
                w = []
                if c == 0:
                    w = [orl[st], ev["wo_done"], ev["wo_done2"]]
                    w.append(ffn_free[NT - 2 + st] if st < 2 else wo_free[st - 2])
                for nsl in range(4):
                    mm = pr.op("tensor", (lambda st, c, nsl, h, b: lambda e: e.matmul(
                        ps[h][:, nsl * 512:(nsl + 1) * 512],
                        wpn[b][:, c * P:(c + 1) * P],
                        buf2[:, c * D + nsl * 512:c * D + (nsl + 1) * 512],
                        start=(c == 0), stop=(c == NT - 1)))(st, c, nsl, h, b),
                        waits=w if nsl == 0 else (),
                        inc=("tt" if (c == NT - 1 and nsl == 3) else None))
            sgm = pr.op("vector", (lambda b, h: lambda e: e.tensor_mul(
                escr[:, :], ps[h][:, :], evc[b][:, :]))(b, h),
                waits=[mm, grl[st]] + ([ffn_mm[NT - 1]] if st == 0 else []))
            wo_free[st] = sgm
            vadd[st] = pr.op("vector", (lambda st, b: lambda e: e.tensor_add(
                ott[b][:, :], escr[:, :], xt[b][:, :]))(st, b),
                waits=[xrl[st]] + ([out_ev[st - 2]] if st >= 2
                                   else [apl[12 + st], x8sp[14 + st],
                                         ffn_mm[NT - 1]]))
            out_ev[st] = pr.op("scalar", (lambda st, b: lambda e: e.dma_start(
                out=out_hbm[st * P:(st + 1) * P, :], in_=ott[b][:, :]))(st, b),
                waits=[vadd[st]], inc=s2("pout", st), n=16)
            if st + 2 < NT:
                orl[st + 2] = pr.op("sync", (lambda st, b: lambda e: e.dma_start(
                    out=wpn[b][:, :], in_=o_sp[st + 2]))(st, b),
                    waits=[mm], inc=s2("so", st), n=16)
                grl[st + 2] = pr.op("sync", (lambda st, b: lambda e: e.dma_start(
                    out=evc[b][:, :], in_=g_sp[st + 2]))(st, b),
                    waits=[sgm], inc=s2("sg2", st), n=16)
                xrl[st + 2] = pr.op("sync", (lambda st, b: lambda e: e.dma_start(
                    out=xt[b][:, :], in_=x_hbm[(st + 2) * P:(st + 3) * P, :]))(st, b),
                    waits=[vadd[st]], inc=f"rs{st % 2}", n=16)

        final_waits = [("pout", pr.counts["pout"]), ("pout2", pr.counts["pout2"])]
        if debug:
            for s in ("pv", "pv2", "pq", "pq2", "po", "po2", "pg", "pg2",
                      "px8", "px82", "pw8", "pw82"):
                final_waits.append((s, pr.counts[s]))
        pr.op("sync", lambda e: e.nop(), waits=final_waits)

        @block.sync
        def _(e):
            pr.emit(e, "sync", handles)

        @block.gpsimd
        def _(e):
            pr.emit(e, "gpsimd", handles)

        @block.vector
        def _(e):
            pr.emit(e, "vector", handles)

        @block.scalar
        def _(e):
            pr.emit(e, "scalar", handles)

        @block.tensor
        def _(e):
            pr.emit(e, "tensor", handles)

    return nc


def _layout_inputs(L0_attn_x, L0_attn_Wq, L0_attn_Wk, L0_attn_Wv, L0_attn_Wo,
                   L0_ffn_weight):
    def panelize(w):
        return np.ascontiguousarray(
            np.asarray(w).reshape(NT, P, NT, P).transpose(2, 1, 0, 3).reshape(NT, P, D))

    wqp = panelize(L0_attn_Wq)
    wkp = panelize(L0_attn_Wk)
    wvp = panelize(L0_attn_Wv)
    wo = np.ascontiguousarray(np.asarray(L0_attn_Wo))
    wft = np.ascontiguousarray(np.asarray(L0_ffn_weight).T)
    xs = np.asarray(L0_attn_x)
    return [{"x": np.ascontiguousarray(xs[c]), "wqp": wqp, "wkp": wkp,
             "wvp": wvp, "wo": wo, "wft": wft} for c in range(NCORES)]


def kernel(L0_attn_x, L0_attn_Wq, L0_attn_Wk, L0_attn_Wv, L0_attn_Wo,
           L0_ffn_weight):
    in_maps = _layout_inputs(L0_attn_x, L0_attn_Wq, L0_attn_Wk, L0_attn_Wv,
                             L0_attn_Wo, L0_ffn_weight)
    nc = build_nc(debug=DEBUG)
    res = run_bass_kernel_spmd(nc, in_maps, core_ids=list(range(NCORES)))
    return np.stack([res.results[c]["out"] for c in range(NCORES)], axis=0)
